# version 03
# baseline (speedup 1.0000x reference)
import contextlib
import os
import numpy as np

import concourse.bass as bass
import concourse.mybir as mybir
from concourse.bass_utils import run_bass_kernel_spmd

F32 = mybir.dt.float32
F16 = mybir.dt.float16
BF16 = mybir.dt.bfloat16
AF = mybir.ActivationFunctionType
ALU = mybir.AluOpType
AX = mybir.AxisListType

NCORES = 8
TOK = 4096
DM = 1024
KC = 8
RG = [list(range(NCORES))]
ENGS = ("sync", "tensor", "vector", "scalar", "gpsimd")
RS = 256.0
IRS = 1.0 / RS

LAUNCH_GROUPS = [[0, 1, 2, 3]]
KDBG = int(os.environ.get("KDBG", "0"))


class Plan:
    def __init__(self, nc, es):
        self.nc = nc
        self.es = es
        self.streams = {e: [] for e in ENGS}
        self.sems = {}
        self.cnt = {}

    def sem(self, name):
        if name not in self.sems:
            self.sems[name] = self.es.enter_context(self.nc.semaphore(name))
            self.cnt[name] = 0
        return self.sems[name]

    def plan(self, eng, builder, inc=None, d=1):
        v = None
        h = None
        if inc is not None:
            h = self.sem(inc)
            self.cnt[inc] += d
            v = self.cnt[inc]

        def f(E, b=builder, h=h, d=d):
            r = b(E)
            if h is not None:
                r.then_inc(h, d)
        self.streams[eng].append(f)
        return v

    def wait(self, eng, sname, val):
        h = self.sem(sname)
        self.streams[eng].append(lambda E, h=h, v=val: E.wait_ge(h, v))

    def sem_inc(self, eng, sname, d=1):
        h = self.sem(sname)
        self.cnt[sname] += d
        v = self.cnt[sname]
        self.streams[eng].append(lambda E, h=h, d=d: E.sem_inc(h, d))
        return v

    def vs(self):
        self.wait("vector", "dv", self.cnt["dv"])

    def bar(self, engs=ENGS):
        for e, sn in (("vector", "dv"), ("scalar", "ac")):
            if e in engs and sn in self.cnt:
                self.wait(e, sn, self.cnt[sn])
        vals = {}
        for e in engs:
            vals[e] = self.sem_inc(e, "bar_" + e)
        for e in engs:
            for o in engs:
                if o != e:
                    self.wait(e, "bar_" + o, vals[o])

    def cbar(self):
        self.bar(("tensor", "vector", "scalar"))



_UID = [0]


def _sbuf(nc, name, shape, dtype):
    _UID[0] += 1
    return nc.sbuf_tensor("%s_u%d" % (name, _UID[0]), shape, dtype)


def _psum(nc, name, shape, dtype):
    _UID[0] += 1
    return nc.psum_tensor("%s_u%d" % (name, _UID[0]), shape, dtype)


def _mm(p, out, lhsT, rhs, start, stop, inc=None, skip=False):
    return p.plan("tensor",
                  lambda E, o=out, l=lhsT, r=rhs, s1=start, s2=stop, sk=skip:
                  E.matmul(o, l, r, start=s1, stop=s2, skip_group_check=sk), inc=inc)


def _g_dma(p, out, in_, inc="gw"):
    return p.plan("gpsimd", lambda E, o=out, i=in_: E.dma_start(out=o, in_=i),
                  inc=inc, d=16)


def _s_dma(p, out, in_, inc="sx"):
    return p.plan("sync", lambda E, o=out, i=in_: E.dma_start(out=o, in_=i),
                  inc=inc, d=16)


def _act(p, out, in_, func, bias=None, scale=None, inc=None):
    kw = {}
    if bias is not None:
        kw["bias"] = bias
    if scale is not None:
        kw["scale"] = scale
    return p.plan("scalar",
                  lambda E, o=out, i=in_, f=func, kw=dict(kw): E.activation(o, i, f, **kw),
                  inc=inc)


def _tt(p, out, a, b, op, inc=None):
    return p.plan("vector",
                  lambda E, o=out, x=a, y=b, op=op: E.tensor_tensor(o, x, y, op), inc=inc)


def _ts(p, out, a, s1, s2, op0, op1, inc=None):
    return p.plan("vector",
                  lambda E, o=out, x=a, s1=s1, s2=s2, o0=op0, o1=op1:
                  E.tensor_scalar(o, x, s1, s2, o0, o1), inc=inc)


def _stt(p, out, in0, scalar, in1, op0, op1, inc=None):
    return p.plan("vector",
                  lambda E, o=out, a=in0, s=scalar, b=in1, o0=op0, o1=op1:
                  E.scalar_tensor_tensor(o, a, s, b, o0, o1), inc=inc)


def _cp(p, out, in_, inc=None):
    return p.plan("vector", lambda E, o=out, i=in_: E.tensor_copy(o, i), inc=inc)


def _red(p, out, in_, op, inc=None):
    return p.plan("vector",
                  lambda E, o=out, i=in_, op=op: E.tensor_reduce(o, i, AX.X, op), inc=inc)


def _tr(p, out, in_, ident, inc=None):
    return p.plan("tensor",
                  lambda E, o=out, i=in_, d=ident: E.transpose(o, i, d), inc=inc)




def emit_init(p, B, need):
    nc = p.nc
    p.plan("vector", lambda E, t=B.ones_col: E.memset(t[:, :], 1.0))
    p.plan("vector", lambda E, t=B.c_eps: E.memset(t[:, :], 1e-5 / (RS * RS)))
    p.plan("vector", lambda E, t=B.c_one: E.memset(t[:, :], 1.0))
    p.plan("vector", lambda E, t=B.ones_row: E.memset(t[:, :], 1.0), inc="dv")
    _g_dma(p, B.ident[:, :], B.d["ident"])
    _g_dma(p, B.sel8[:, :], B.d["sel8"])
    xr = B.d["x"].rearrange("(k p) n -> p k n", p=128)
    small = [(B.bselr, "bsel"), (B.hselr, "hsel")]
    if 0 in need:
        small += [(B.bqc, "bq"), (B.bkc, "bk"), (B.bvc, "bv"), (B.boc, "bo")]
    if 1 in need:
        small += [(B.l0b1c, "l0b1"), (B.l0b2c, "l0b2"), (B.l0sb1c, "l0sb1"),
                  (B.l0sb2c, "l0sb2")]
        _g_dma(p, B.gwT[:, :, :], B.d["gwT"])
    if 2 in need:
        small += [(B.convwc, "convw"), (B.convbc, "convb"), (B.bdtc, "bdt")]
        _g_dma(p, B.wxT[:, :, :], B.d["wxT"])
        _g_dma(p, B.wdtT[:, :], B.d["wdtT"])
    if 3 in need:
        _g_dma(p, B.wgT[:, :, :], B.d["wgT"])
    for t, d_name in small:
        _s_dma(p, t.ap(), B.d[d_name])
    p.wait("gpsimd", "gw", p.cnt["gw"])
    p.wait("sync", "sx", p.cnt["sx"])
    p.bar()
    with _psum(nc, "ps_init", [128, 16], F32) as ps_init:
        _cp(p, B.row16[0:1, 0, 0:4], B.bselr[0:1, :])
        _cp(p, B.row16[0:1, 0, 4:12], B.hselr[0:1, :], inc="dv")
        p.wait("tensor", "dv", p.cnt["dv"])
        _mm(p, ps_init[:, 0:12], B.ones_row[0:1, :], B.row16[0:1, 0, 0:12], True, True,
            inc="mm")
        p.wait("vector", "mm", p.cnt["mm"])
        _cp(p, B.bsel_bc[:, :], ps_init[:, 0:4])
        _cp(p, B.hsel_bc[:, :], ps_init[:, 4:12], inc="dv")
        if 0 in need:
            _ts(p, B.negbq[:, :], B.bqc[:, :], -1.0, None, ALU.mult, ALU.bypass)
            _ts(p, B.negbk[:, :], B.bkc[:, :], -1.0, None, ALU.mult, ALU.bypass)
            _ts(p, B.boS[:, :], B.boc[:, :], IRS, None, ALU.mult, ALU.bypass, inc="dv")
        if 1 in need:
            _ts(p, B.l0sb2S[:, :], B.l0sb2c[:, :], IRS, None, ALU.mult, ALU.bypass,
                inc="dv")
        p.wait("tensor", "dv", p.cnt["dv"])
    with _sbuf(nc, "xstage", [128, 2, 512], F32) as xstage:
        for k in range(KC):
            for t in range(8):
                i = k * 8 + t
                slot = i % 2
                if i >= 2:
                    p.wait("sync", "dv", p.cnt["dv"] - 1)
                sv = _s_dma(p, xstage[:, slot, :], xr[:, k, t * 512:(t + 1) * 512],
                            inc="lx%d" % slot)
                p.wait("vector", "lx%d" % slot, sv)
                _ts(p, B.xres[:, k, t * 512:(t + 1) * 512], xstage[:, slot, :],
                    IRS, None, ALU.mult, ALU.bypass, inc="dv")
        p.wait("sync", "dv", p.cnt["dv"])
    p.bar()


def emit_ln(p, B, src, src_off, tok_cnt, xn, xn_off, pad=0, cb=False):
    nc = p.nc
    tiles = []
    t0 = 0
    while t0 < tok_cnt:
        w = min(512, tok_cnt - t0)
        tiles.append((t0, max(w, pad)))
        t0 += w
    with _psum(nc, "ln_ps", [128, 4, 512], F32) as ps, \
            _sbuf(nc, "ln_sq", [128, KC, 512], BF16) as sq:
        for (t0, w) in tiles:
            sl = slice(src_off + t0, src_off + t0 + w)
            so = slice(xn_off + t0, xn_off + t0 + w)
            p.wait("scalar", "mm", p.cnt["mm"])
            for k in range(KC):
                _act(p, sq[:, k, 0:w], src[:, k, sl], AF.Square,
                     inc="ac" if k == KC - 1 else None)
            p.wait("tensor", "ac", p.cnt["ac"])
            p.wait("tensor", "dv", p.cnt["dv"])
            for k in range(KC):
                _mm(p, ps[0:1, 0, 0:w], B.ones_col[:, 0:1], src[:, k, sl],
                    k == 0, k == KC - 1)
            for k in range(KC):
                _mm(p, ps[0:1, 1, 0:w], B.ones_col[:, 0:1], sq[:, k, 0:w],
                    k == 0, k == KC - 1, inc="mm" if k == KC - 1 else None)
            p.wait("vector", "mm", p.cnt["mm"])
            _ts(p, B.srow[0:1, 0, 0:w], ps[0:1, 0, 0:w], 1.0 / DM, None,
                ALU.mult, ALU.bypass, inc="dv")
            _ts(p, B.srow[0:1, 1, 0:w], ps[0:1, 1, 0:w], 1.0 / DM, None,
                ALU.mult, ALU.bypass, inc="dv"); p.vs()
            _tt(p, B.srow[0:1, 2, 0:w], B.srow[0:1, 0, 0:w], B.srow[0:1, 0, 0:w],
                ALU.mult, inc="dv"); p.vs()
            _tt(p, B.srow[0:1, 1, 0:w], B.srow[0:1, 1, 0:w], B.srow[0:1, 2, 0:w],
                ALU.subtract, inc="dv"); p.vs()
            p.wait("scalar", "dv", p.cnt["dv"])
            _act(p, B.srow[0:1, 1, 0:w], B.srow[0:1, 1, 0:w], AF.Sqrt, bias=B.c_eps[0:1, 0:1],
                 inc="ac")
            p.wait("vector", "ac", p.cnt["ac"])
            p.plan("vector", lambda E, o=B.srow[0:1, 2, 0:w], i=B.srow[0:1, 1, 0:w]:
                   E.reciprocal(o, i), inc="dv")
            p.vs()
            _ts(p, B.srow[0:1, 2, 0:w], B.srow[0:1, 2, 0:w], 60000.0, None,
                ALU.min, ALU.bypass, inc="dv"); p.vs()
            _cp(p, B.row16[0:1, 0, 0:w], B.srow[0:1, 0, 0:w])
            _cp(p, B.row16[0:1, 1, 0:w], B.srow[0:1, 2, 0:w], inc="dv")
            p.wait("tensor", "dv", p.cnt["dv"])
            _mm(p, ps[:, 2, 0:w], B.ones_row[0:1, :], B.row16[0:1, 0, 0:w], True, True)
            _mm(p, ps[:, 3, 0:w], B.ones_row[0:1, :], B.row16[0:1, 1, 0:w], True, True,
                inc="mm")
            p.wait("vector", "mm", p.cnt["mm"])
            for k in range(KC):
                _tt(p, B.tmpf[:, 0:w], src[:, k, sl], ps[:, 2, 0:w], ALU.subtract)
                _tt(p, xn[:, k, so], B.tmpf[:, 0:w], ps[:, 3, 0:w], ALU.mult,
                    inc="dv" if k == KC - 1 else None)
        p.wait("tensor", "dv", p.cnt["dv"])
    if cb:
        p.cbar()
    else:
        p.bar()


def _elu1_evac(p, B, ps_tile, negb_col, posb_col, out16, w):
    p.wait("scalar", "dv", p.cnt["dv"])
    _act(p, B.e1[:, 0:w], ps_tile, AF.Relu, bias=negb_col, scale=-1.0)
    _act(p, B.e1[:, 0:w], B.e1[:, 0:w], AF.Exp, scale=-1.0)
    _act(p, B.e2[:, 0:w], ps_tile, AF.Relu, bias=posb_col, inc="ac")
    p.wait("vector", "ac", p.cnt["ac"])
    return _tt(p, out16, B.e1[:, 0:w], B.e2[:, 0:w], ALU.add, inc="dv")


def emit_l0_attn(p, B):
    nc = p.nc
    with contextlib.ExitStack() as es:
        xn = es.enter_context(_sbuf(nc, "xn_full", [128, KC, TOK], F16))
        emit_ln(p, B, B.xres, 0, TOK, xn, 0)
        if KDBG:
            dxn = B.d["dbg_xn"].rearrange("(k p) n -> p k n", p=128)
            for k in range(KC):
                _s_dma(p, dxn[:, k, :], xn[:, k, :], inc="dbq")
            p.wait("sync", "dbq", p.cnt["dbq"])
            p.bar()

        wa = es.enter_context(_sbuf(nc, "wa", [128, 2, KC, 512], F16))
        k_sb = es.enter_context(_sbuf(nc, "k_sb", [128, 4, 512], F16))
        v_sb = es.enter_context(_sbuf(nc, "v_sb", [128, 4, 512], F16))
        ktok = es.enter_context(_sbuf(nc, "ktok", [128, 4, 4, 128], F16))
        vtok = es.enter_context(_sbuf(nc, "vtok", [128, 4, 4, 128], F16))
        kv16 = es.enter_context(_sbuf(nc, "kv16", [128, 8, 64], F16))
        ps_r = es.enter_context(_psum(nc, "ps_r", [128, 4, 512], F32))
        ps_kv = es.enter_context(_psum(nc, "ps_kv", [128, 512], F32))
        psT = es.enter_context(_psum(nc, "psT", [128, 4, 128], F16))

        wkT = B.d["wkT"].rearrange("(k p) m -> p k m", p=128)
        wvT = B.d["wvT"].rearrange("(k p) m -> p k m", p=128)
        wqT = B.d["wqT"].rearrange("(k p) m -> p k m", p=128)
        woT = B.d["woT"].rearrange("(k p) m -> p k m", p=128)

        def proj_tile(wslot, mc, tsl, slot):
            for k in range(KC):
                _mm(p, ps_r[:, slot, :], wa[:, wslot, k, mc * 128:(mc + 1) * 128],
                    xn[:, k, tsl], k == 0, k == KC - 1,
                    inc="mm" if k == KC - 1 else None, skip=True)

        first_kv = [True]
        n_kv = 2 * 8 * 4 * 8
        kv_done = [0]
        with _sbuf(nc, "kvf", [128, 512], F32) as kvf, \
                _sbuf(nc, "kvz", [128, 4, 8, 64], F32) as kvz:
            for mh in range(2):
                csl = slice(mh * 512, (mh + 1) * 512)
                p.wait("gpsimd", "mm", B.wfree.get("attn_w", 0))
                _g_dma(p, wa[:, 0, :, :], wkT[:, :, csl], inc="wat")
                gv = _g_dma(p, wa[:, 1, :, :], wvT[:, :, csl], inc="wat")
                p.wait("tensor", "wat", gv)
                for t in range(8):
                    tsl = slice(t * 512, (t + 1) * 512)
                    for mc in range(4):
                        m_g = mh * 4 + mc
                        p.wait("tensor", "dv", p.cnt["dv"])
                        proj_tile(0, mc, tsl, 0)
                        p.wait("scalar", "mm", p.cnt["mm"])
                        _elu1_evac(p, B, ps_r[:, 0, :], B.negbk[:, m_g:m_g + 1],
                                   B.bkc[:, m_g:m_g + 1], k_sb[:, mc, :], 512)
                        p.wait("tensor", "dv", p.cnt["dv"])
                        proj_tile(1, mc, tsl, 1)
                        p.wait("vector", "mm", p.cnt["mm"])
                        _ts(p, v_sb[:, mc, :], ps_r[:, 1, :], B.bvc[:, m_g:m_g + 1],
                            None, ALU.add, ALU.bypass, inc="dv")
                    B.wfree["attn_w"] = p.cnt["mm"]
                    p.wait("tensor", "dv", p.cnt["dv"])
                    for mc in range(4):
                        for tc in range(4):
                            _tr(p, psT[:, 0, :], k_sb[:, mc, tc * 128:(tc + 1) * 128],
                                B.ident[:, :])
                            _tr(p, psT[:, 1, :], v_sb[:, mc, tc * 128:(tc + 1) * 128],
                                B.ident[:, :], inc="mm")
                            p.wait("vector", "mm", p.cnt["mm"])
                            _cp(p, ktok[:, tc, mc, :], psT[:, 0, :])
                            dvv = _cp(p, vtok[:, tc, mc, :], psT[:, 1, :], inc="dv")
                            p.wait("tensor", "dv", dvv)
                    for tc in range(4):
                        for j in range(8):
                            h_g = mh * 8 + j
                            mc, hh = j // 2, j % 2
                            kv_done[0] += 1
                            _mm(p, ps_kv[hh * 64:(hh + 1) * 64,
                                         (h_g // 2) * 64:(h_g // 2) * 64 + 64],
                                ktok[:, tc, mc, hh * 64:(hh + 1) * 64],
                                vtok[:, tc, mc, hh * 64:(hh + 1) * 64],
                                first_kv[0], kv_done[0] == n_kv,
                                inc="mm" if kv_done[0] == n_kv else None, skip=True)
                            first_kv[0] = False
            p.wait("vector", "mm", p.cnt["mm"])
            _cp(p, kvf[:, :], ps_kv[:, :], inc="dv"); p.vs()
            kvf_v = kvf.rearrange("p (j e) -> p j e", j=8)
            for b in range(4):
                _ts(p, kvz[:, b, :, :], kvf_v[:, :, :], B.bsel_bc[:, b:b + 1], None,
                    ALU.mult, ALU.bypass, inc="dv" if b == 3 else None)
            p.wait("gpsimd", "dv", p.cnt["dv"])
            kvin = B.kv_in_d.rearrange("p (b j e) -> p b j e", b=4, j=8)
            kvout = B.kv_out_d.rearrange("p (b j e) -> p b j e", b=4, j=8)
            kq = _g_dma(p, kvin[:, :, :, :], kvz[:, :, :, :], inc="kvq")
            p.wait("gpsimd", "kvq", kq)
            p.plan("gpsimd", lambda E: E.collective_compute(
                "AllReduce", ALU.add, RG, [B.kv_in_d[:, :].opt()],
                [B.kv_out_d[:, :].opt()]), inc="cc")
            p.wait("gpsimd", "cc", p.cnt["cc"])
            p.wait("gpsimd", "mm", B.wfree.get("attn_w", 0))
            for mh in range(2):
                wav = _g_dma(p, wa[:, mh, :, :], wqT[:, :, mh * 512:(mh + 1) * 512],
                             inc="wat")
            gv = _g_dma(p, kvz[:, :, :, :], kvout[:, :, :, :], inc="kvq")
            p.wait("vector", "kvq", gv)
            _ts(p, kvf_v[:, :, :], kvz[:, 0, :, :], B.bsel_bc[:, 0:1], None,
                ALU.mult, ALU.bypass, inc="dv"); p.vs()
            for b in range(1, 4):
                _stt(p, kvf_v[:, :, :], kvz[:, b, :, :], B.bsel_bc[:, b:b + 1],
                     kvf_v[:, :, :], ALU.mult, ALU.add, inc="dv"); p.vs()
            _cp(p, kv16[:, :, :], kvf_v[:, :, :], inc="dv")
            p.wait("tensor", "wat", wav)
            if KDBG:
                p.wait("sync", "dv", p.cnt["dv"])
                _s_dma(p, B.d["dbg_kv"][:, :], kvf[:, :], inc="dbq")
                _s_dma(p, B.d["dbg_kv16"][:, :, :], kv16[:, :, :], inc="dbq")
                p.wait("sync", "dbq", p.cnt["dbq"])
                p.bar()
        p.bar()

        with _sbuf(nc, "wo_b", [128, 2, KC, 512], F16) as wo_b:
            for mh in range(2):
                wov = _g_dma(p, wo_b[:, mh, :, :], woT[:, :, mh * 512:(mh + 1) * 512],
                             inc="wob")
            p.wait("tensor", "wob", wov)
            vtok_flat = vtok.rearrange("p a b c -> p (a b c)")
            ktok_flat = ktok.rearrange("p a b c -> p (a b c)")

            def attn_sl(m, rows=slice(0, 128)):
                if m < 4:
                    return v_sb[rows, m, :]
                return vtok_flat[rows, (m - 4) * 512:(m - 3) * 512]
            for t in range(8):
                tsl = slice(t * 512, (t + 1) * 512)
                for m_g in range(8):
                    mh, mc = m_g // 4, m_g % 4
                    p.wait("tensor", "dv", p.cnt["dv"])
                    proj_tile(mh, mc, tsl, m_g % 2)
                    p.wait("scalar", "mm", p.cnt["mm"])
                    if mh == 0:
                        out16 = k_sb[:, mc, :]
                    else:
                        out16 = ktok_flat[:, mc * 512:(mc + 1) * 512]
                    _elu1_evac(p, B, ps_r[:, m_g % 2, :], B.negbq[:, m_g:m_g + 1],
                               B.bqc[:, m_g:m_g + 1], out16, 512)
                p.wait("tensor", "dv", p.cnt["dv"])
                if KDBG and t == 0:
                    for m_g in range(8):
                        src = (k_sb[:, m_g, :] if m_g < 4 else
                               ktok_flat[:, (m_g - 4) * 512:(m_g - 3) * 512])
                        _s_dma(p, B.d["dbg_q"][:, m_g, :], src, inc="dbq")
                    p.wait("sync", "dbq", p.cnt["dbq"])
                    p.bar()
                for mpair in range(8):
                    p.wait("tensor", "dv", p.cnt["dv"])
                    for hh in range(2):
                        h_g = mpair * 2 + hh
                        m_g = h_g // 2
                        if m_g < 4:
                            q_sl = k_sb[hh * 64:(hh + 1) * 64, m_g, :]
                        else:
                            q_sl = ktok_flat[hh * 64:(hh + 1) * 64,
                                             (m_g - 4) * 512:(m_g - 3) * 512]
                        _mm(p, ps_r[hh * 64:(hh + 1) * 64, 2 + hh, :],
                            kv16[hh * 64:(hh + 1) * 64, h_g // 2, :], q_sl,
                            True, True, inc="mm" if hh == 1 else None, skip=True)
                    p.wait("vector", "mm", p.cnt["mm"])
                    _ts(p, attn_sl(mpair, slice(0, 64)), ps_r[0:64, 2, :], IRS, None,
                        ALU.mult, ALU.bypass)
                    _ts(p, attn_sl(mpair, slice(64, 128)), ps_r[64:128, 3, :], IRS,
                        None, ALU.mult, ALU.bypass, inc="dv")
                p.wait("tensor", "dv", p.cnt["dv"])
                if KDBG and t == 0:
                    for m_g in range(8):
                        _s_dma(p, B.d["dbg_attn"][:, m_g, :], attn_sl(m_g), inc="dbq")
                    p.wait("sync", "dbq", p.cnt["dbq"])
                    p.bar()
                for m2 in range(8):
                    slot = m2 % 2
                    p.wait("tensor", "dv", p.cnt["dv"])
                    for k in range(KC):
                        _mm(p, ps_r[:, slot, :],
                            wo_b[:, m2 // 4, k, (m2 % 4) * 128:(m2 % 4 + 1) * 128],
                            attn_sl(k), k == 0, k == KC - 1,
                            inc="mm" if k == KC - 1 else None, skip=True)
                    p.wait("vector", "mm", p.cnt["mm"])
                    _stt(p, B.xres[:, m2, tsl], ps_r[:, slot, :], B.boS[:, m2:m2 + 1],
                         B.xres[:, m2, tsl], ALU.add, ALU.add, inc="dv")
            B.wfree["attn_w"] = p.cnt["mm"]
            p.wait("tensor", "dv", p.cnt["dv"])
    p.bar()


def _wload(p, B, key, buf, dst, s32, s16, q, par):
    if q == 0:
        p.wait("gpsimd", "mm", B.wfree.get((key, buf), 0))
        wbv = B.wbfree.get((key, buf))
        if wbv:
            p.wait("gpsimd", "wb", wbv)
        v = p.plan("gpsimd", lambda E, o=dst, i=s32: E.dma_start(out=o, in_=i),
                   inc="mwg", d=16)
        p.wait("sync", "mwg", v)
        wv = p.plan("sync", lambda E, o=s16, i=dst: E.dma_start(out=o, in_=i),
                    inc="wb", d=16)
        B.wbfree[(key, buf)] = wv
        return ("mwg", v)
    eng, semn = ("gpsimd", "mwg") if par == 0 else ("sync", "mws")
    p.wait(eng, "mm", B.wfree.get((key, buf), 0))
    v = p.plan(eng, lambda E, o=dst, i=s16: E.dma_start(out=o, in_=i),
               inc=semn, d=16)
    return (semn, v)


def emit_moe(p, B, which):
    nc = p.nc
    geglu = which == "l1"
    with contextlib.ExitStack() as es_w:
        w1b = es_w.enter_context(_sbuf(nc, "w1b", [128, 2, KC, 256], F16))
        if geglu:
            w2b = es_w.enter_context(_sbuf(nc, "w2b", [128, 2, KC, 256], F16))
        w3b = es_w.enter_context(_sbuf(nc, "w3b", [128, 4, 2, 512], F16))
        for q in range(4):
            q0 = q * 1024
            if q == 1:
                p.wait("gpsimd", "wb", p.cnt["wb"])
                p.wait("sync", "wb", p.cnt["wb"])
            with contextlib.ExitStack() as es:
                xn = es.enter_context(_sbuf(nc, "xn_q", [128, KC, 1024], F16))
                emit_ln(p, B, B.xres, q0, 1024, xn, 0, cb=True)
                cfm = es.enter_context(_sbuf(nc, "cfm", [8, 1024], F16))
                mask = es.enter_context(_sbuf(nc, "mask", [128, 2, 512], F32))
                h_sb = es.enter_context(_sbuf(nc, "h_sb", [128, 32, 1024], F16))
                if geglu:
                    tg = es.enter_context(_sbuf(nc, "tg", [128, 2, 512], F16))

                gw_l = B.gwT if which == "l0" else B.wgT
                with _psum(nc, "ps_g", [128, 2, 512], F32) as ps, \
                        _psum(nc, "psTg", [128, 4, 128], F16) as psT, \
                        _sbuf(nc, "lg", [8, 1024], F32) as lg, \
                        _sbuf(nc, "lg16", [8, 1024], F16) as lg16, \
                        _sbuf(nc, "ltok", [128, 8, 8], F32) as ltok, \
                        _sbuf(nc, "cmb", [128, 8, 8], F32) as cmb, \
                        _sbuf(nc, "cmb16", [128, 8, 8], F16) as cmb16, \
                        _sbuf(nc, "colA", [128, 4], F32) as colA:
                    for t2 in range(2):
                        tsl = slice(t2 * 512, (t2 + 1) * 512)
                        for k in range(KC):
                            _mm(p, ps[0:8, t2, :], gw_l[:, k, :], xn[:, k, tsl],
                                k == 0, k == KC - 1, inc="mm" if k == KC - 1 else None)
                    p.wait("vector", "mm", p.cnt["mm"])
                    _cp(p, lg[:, 0:512], ps[0:8, 0, :])
                    _cp(p, lg[:, 512:1024], ps[0:8, 1, :])
                    _cp(p, lg16[:, :], lg[:, :], inc="dv")
                    p.wait("tensor", "dv", p.cnt["dv"])
                    for tc in range(8):
                        slot = tc % 4
                        p.wait("tensor", "dv", p.cnt["dv"])
                        _tr(p, psT[:, slot, 0:8], lg16[0:8, tc * 128:(tc + 1) * 128],
                            B.ident[0:8, 0:8], inc="mm")
                        p.wait("vector", "mm", p.cnt["mm"])
                        _cp(p, ltok[:, tc, :], psT[:, slot, 0:8], inc="dv")
                    for tc in range(8):
                        lt = ltok[:, tc, :]
                        _red(p, colA[:, 0:1], lt, ALU.max, inc="dv"); p.vs()
                        if which == "l0":
                            _ts(p, cmb[:, tc, :], lt, colA[:, 0:1], 0.0, ALU.subtract,
                                ALU.is_equal, inc="dv"); p.vs()
                            _stt(p, B.tmpf[:, 0:8], cmb[:, tc, :], -1e30, lt,
                                 ALU.mult, ALU.add, inc="dv"); p.vs()
                            _red(p, colA[:, 1:2], B.tmpf[:, 0:8], ALU.max, inc="dv"); p.vs()
                            _tt(p, colA[:, 2:3], colA[:, 0:1], colA[:, 1:2], ALU.subtract,
                                inc="dv")
                            p.wait("scalar", "dv", p.cnt["dv"])
                            _act(p, colA[:, 3:4], colA[:, 2:3], AF.Sigmoid, inc="ac")
                            p.wait("vector", "ac", p.cnt["ac"])
                            _ts(p, B.tmpf[:, 8:16], lt, colA[:, 1:2], 0.0, ALU.subtract,
                                ALU.is_equal, inc="dv"); p.vs()
                            _ts(p, colA[:, 2:3], colA[:, 3:4], -1.0, 1.0, ALU.mult,
                                ALU.add, inc="dv"); p.vs()
                            _ts(p, cmb[:, tc, :], cmb[:, tc, :], colA[:, 3:4], None,
                                ALU.mult, ALU.bypass, inc="dv"); p.vs()
                            _stt(p, cmb[:, tc, :], B.tmpf[:, 8:16], colA[:, 2:3],
                                 cmb[:, tc, :], ALU.mult, ALU.add, inc="dv"); p.vs()
                        else:
                            _ts(p, cmb[:, tc, :], lt, colA[:, 0:1], None, ALU.is_ge,
                                ALU.bypass, inc="dv"); p.vs()
                    _ts(p, cmb[:, :, :], cmb[:, :, :], IRS, None, ALU.mult, ALU.bypass,
                        inc="dv"); p.vs()
                    _cp(p, cmb16[:, :, :], cmb[:, :, :], inc="dv")
                    p.wait("tensor", "dv", p.cnt["dv"])
                    for tc in range(8):
                        slot = tc % 4
                        p.wait("tensor", "dv", p.cnt["dv"])
                        _tr(p, psT[0:8, slot, :], cmb16[:, tc, :], B.ident[:, :], inc="mm")
                        p.wait("vector", "mm", p.cnt["mm"])
                        _cp(p, cfm[:, tc * 128:(tc + 1) * 128], psT[0:8, slot, :],
                            inc="dv")
                    p.wait("tensor", "dv", p.cnt["dv"])

                with _psum(nc, "ps8", [128, 8, 512], F32) as ps8:
                    for e in range(9):
                        shared = e == 8
                        if which == "l0":
                            w1_src = B.d["l0sw1"] if shared else B.d["l0w1"][e]
                            w1_f = B.d16["l0sw1"] if shared else B.d16["l0w1"][e]
                            b1c = B.l0sb1c if shared else B.l0b1c
                            w3_src = B.d["l0sw2"] if shared else B.d["l0w2"][e]
                            w3_f = B.d16["l0sw2"] if shared else B.d16["l0w2"][e]
                            b2c = B.l0sb2c if shared else B.l0b2c
                        else:
                            w1_src = B.d["w1s"] if shared else B.d["l1w1"][e]
                            w1_f = B.d16["w1s"] if shared else B.d16["l1w1"][e]
                            w2_src = B.d["w2s"] if shared else B.d["l1w2"][e]
                            w2_f = B.d16["w2s"] if shared else B.d16["l1w2"][e]
                            w3_src = B.d["w3s"] if shared else B.d["l1w3"][e]
                            w3_f = B.d16["w3s"] if shared else B.d16["l1w3"][e]
                        w1r = w1_src.rearrange("(k p) m -> p k m", p=128)
                        w1fr = w1_f.rearrange("(k p) m -> p k m", p=128)
                        if geglu:
                            w2r = w2_src.rearrange("(k p) m -> p k m", p=128)
                            w2fr = w2_f.rearrange("(k p) m -> p k m", p=128)
                        w3r = w3_src.rearrange("(k p) m -> p k m", p=128)
                        w3fr = w3_f.rearrange("(k p) m -> p k m", p=128)

                        p.wait("tensor", "dv", p.cnt["dv"])
                        if not shared:
                            for t2 in range(2):
                                _mm(p, ps8[:, 4 + t2, :], B.sel8[:, e * 128:(e + 1) * 128],
                                    cfm[0:8, t2 * 512:(t2 + 1) * 512], True, True,
                                    inc="mm" if t2 == 1 else None)
                            p.wait("vector", "mm", p.cnt["mm"])
                            _cp(p, mask[:, 0, :], ps8[:, 4, :])
                            mdv = _cp(p, mask[:, 1, :], ps8[:, 5, :], inc="dv")
                            slotgate = {4: ("dv", mdv), 5: ("dv", mdv)}
                        else:
                            slotgate = {}
                        tggate = {}

                        for c in range(16):
                            buf = c % 2
                            csl = slice(c * 256, (c + 1) * 256)
                            if geglu:
                                s1, v1 = _wload(p, B, "w1", buf, w1b[:, buf, :, :],
                                                w1r[:, :, csl], w1fr[:, :, csl], q, 0)
                                s2, v2 = _wload(p, B, "w2", buf, w2b[:, buf, :, :],
                                                w2r[:, :, csl], w2fr[:, :, csl], q, 1)
                                p.wait("tensor", s1, v1)
                                p.wait("tensor", s2, v2)
                            else:
                                s1, v1 = _wload(p, B, "w1", buf, w1b[:, buf, :, :],
                                                w1r[:, :, csl], w1fr[:, :, csl], q,
                                                c % 2)
                                p.wait("tensor", s1, v1)
                            for mc in range(2):
                                ch = c * 2 + mc
                                for t2 in range(2):
                                    tsl = slice(t2 * 512, (t2 + 1) * 512)
                                    b1 = mc * 2 + t2
                                    if b1 in slotgate:
                                        p.wait("tensor", *slotgate[b1])
                                    if geglu and (4 + b1) in slotgate:
                                        p.wait("tensor", *slotgate[4 + b1])
                                    for k in range(KC):
                                        _mm(p, ps8[:, b1, :],
                                            w1b[:, buf, k, mc * 128:(mc + 1) * 128],
                                            xn[:, k, tsl], k == 0, k == KC - 1,
                                            inc="mm" if (k == KC - 1 and not geglu)
                                            else None, skip=True)
                                    if geglu:
                                        for k in range(KC):
                                            _mm(p, ps8[:, 4 + b1, :],
                                                w2b[:, buf, k, mc * 128:(mc + 1) * 128],
                                                xn[:, k, tsl], k == 0, k == KC - 1,
                                                inc="mm" if k == KC - 1 else None,
                                                skip=True)
                                    mmv = p.cnt["mm"]
                                    p.wait("scalar", "mm", mmv)
                                    if which == "l0":
                                        bias = (b1c[:, ch:ch + 1] if shared
                                                else b1c[:, ch, e:e + 1])
                                        av = _act(p, h_sb[:, ch, tsl], ps8[:, b1, :],
                                                  AF.Gelu, bias=bias, inc="ac")
                                        slotgate[b1] = ("ac", av)
                                    else:
                                        if t2 in tggate:
                                            p.wait("scalar", "dv", tggate[t2])
                                        av = _act(p, tg[:, t2, :], ps8[:, b1, :],
                                                  AF.Gelu, inc="ac")
                                        slotgate[b1] = ("ac", av)
                                        p.wait("vector", "ac", av)
                                        dvv = _tt(p, h_sb[:, ch, tsl], tg[:, t2, :],
                                                  ps8[:, 4 + b1, :], ALU.mult, inc="dv")
                                        slotgate[4 + b1] = ("dv", dvv)
                                        tggate[t2] = dvv
                            B.wfree[("w1", buf)] = p.cnt["mm"]
                            if geglu:
                                B.wfree[("w2", buf)] = p.cnt["mm"]

                        p.wait("tensor", "ac", p.cnt["ac"])
                        p.wait("tensor", "dv", p.cnt["dv"])
                        for mh in range(2):
                            if mh == 1:
                                p.wait("tensor", "dv", p.cnt["dv"])
                            for kg in range(16):
                                buf = kg % 4
                                ssl = (slice(kg * 2, (kg + 1) * 2),
                                       slice(mh * 512, (mh + 1) * 512))
                                s3, v3 = _wload(p, B, "w3", buf, w3b[:, buf, :, :],
                                                w3r[:, ssl[0], ssl[1]],
                                                w3fr[:, ssl[0], ssl[1]], q, kg % 2)
                                p.wait("tensor", s3, v3)
                                for m4 in range(4):
                                    for k in range(2):
                                        for t2 in range(2):
                                            last = (m4 == 3 and k == 1 and t2 == 1)
                                            _mm(p, ps8[:, m4 * 2 + t2, :],
                                                w3b[:, buf, k, m4 * 128:(m4 + 1) * 128],
                                                h_sb[:, kg * 2 + k,
                                                     t2 * 512:(t2 + 1) * 512],
                                                kg == 0 and k == 0, kg == 15 and k == 1,
                                                inc="mm" if last else None, skip=True)
                                B.wfree[("w3", buf)] = p.cnt["mm"]
                            p.wait("vector", "mm", p.cnt["mm"])
                            for m4 in range(4):
                                for t2 in range(2):
                                    m = mh * 4 + m4
                                    psym = ps8[:, m4 * 2 + t2, :]
                                    xsl = B.xres[:, m, q0 + t2 * 512:q0 + (t2 + 1) * 512]
                                    if shared:
                                        if which == "l0":
                                            _ts(p, B.tmpf[:, 0:512], psym, IRS,
                                                B.l0sb2S[:, m:m + 1], ALU.mult, ALU.add)
                                            _tt(p, xsl, xsl, B.tmpf[:, 0:512], ALU.add,
                                                inc="dv")
                                        else:
                                            _stt(p, xsl, psym, IRS, xsl,
                                                 ALU.mult, ALU.add, inc="dv")
                                    else:
                                        if which == "l0":
                                            _stt(p, B.tmpf[:, 0:512], psym,
                                                 b2c[:, m, e:e + 1], mask[:, t2, :],
                                                 ALU.add, ALU.mult)
                                        else:
                                            _stt(p, B.tmpf[:, 0:512], psym, 1.0,
                                                 mask[:, t2, :], ALU.mult, ALU.mult)
                                        _tt(p, xsl, xsl, B.tmpf[:, 0:512], ALU.add,
                                            inc="dv")
    p.bar()



def emit_halo_exchange(p, B):
    nc = p.nc
    with _sbuf(nc, "h32", [128, KC, 3], F32) as h32, \
            _sbuf(nc, "hg", [128, 8, KC, 3], F32) as hg:
        for k in range(KC):
            _cp(p, h32[:, k, :], B.xres[:, k, TOK - 3:TOK],
                inc="dv" if k == KC - 1 else None)
        p.wait("gpsimd", "dv", p.cnt["dv"])
        hin = B.halo_in_d.rearrange("(k p) c -> p k c", p=128)
        hq = _g_dma(p, hin[:, :, :], h32[:, :, :], inc="hlq")
        p.wait("gpsimd", "hlq", hq)
        p.plan("gpsimd", lambda E: E.collective_compute(
            "AllGather", ALU.bypass, RG, [B.halo_in_d[:, :].opt()],
            [B.halo_out_d[:, :].opt()]), inc="cc")
        p.wait("gpsimd", "cc", p.cnt["cc"])
        hout = B.halo_out_d.rearrange("(r k p) c -> p r k c", p=128, k=KC)
        gv = _g_dma(p, hg[:, :, :, :], hout[:, :, :, :], inc="hlq")
        p.wait("vector", "hlq", gv)
        _ts(p, h32[:, :, :], hg[:, 0, :, :], B.hsel_bc[:, 0:1], None,
            ALU.mult, ALU.bypass, inc="dv"); p.vs()
        for r in range(1, 8):
            _stt(p, h32[:, :, :], hg[:, r, :, :], B.hsel_bc[:, r:r + 1],
                 h32[:, :, :], ALU.mult, ALU.add, inc="dv"); p.vs()
        p.plan("vector", lambda E, t=B.halo16: E.memset(t[:, :, :], 0.0), inc="dv")
        p.vs()
        _cp(p, B.halo16[:, :, 0:3], h32[:, :, :], inc="dv")
        if KDBG:
            p.wait("sync", "dv", p.cnt["dv"])
            _s_dma(p, B.d["dbg_h16"][:, :, :], B.halo16[:, :, :], inc="dbq")
            _s_dma(p, B.d["dbg_hsel"][:, :], B.hsel_bc[:, :], inc="dbq")
            p.wait("sync", "dbq", p.cnt["dbq"])
        p.wait("tensor", "dv", p.cnt["dv"])
    p.bar()


def emit_l1_attn(p, B):
    nc = p.nc
    CH = 16
    emit_halo_exchange(p, B)
    winT = B.d["winT"].rearrange("(k p) m -> p k m", p=128)
    woutT = B.d["woutT"].rearrange("(k p) m -> p k m", p=128)
    for hf in range(2):
        h0 = hf * 2048
        hoff = 3 if hf == 0 else 0
        with contextlib.ExitStack() as es:
            xn = es.enter_context(_sbuf(nc, "xn_h", [128, KC, 2051], F16))
            if hf == 0:
                emit_ln(p, B, B.halo16, 0, 3, xn, 0, pad=128)
                if KDBG:
                    _s_dma(p, B.d["dbg_srow"][:, :, :], B.srow[:, :, :], inc="dbq")
                    for k in range(KC):
                        _s_dma(p, B.d["dbg_xnh"][:, k, 0:3], xn[:, k, 0:3], inc="dbq")
                    p.wait("sync", "dbq", p.cnt["dbq"])
                    p.bar()
            emit_ln(p, B, B.xres, h0, 2048, xn, hoff)

            xc = es.enter_context(_sbuf(nc, "xc", [128, CH, 2048], F16))
            es_in = es.enter_context(contextlib.ExitStack())
            xs = es_in.enter_context(_sbuf(nc, "xs", [128, 2051], F16))
            wib = es_in.enter_context(_sbuf(nc, "wib", [128, 2, KC, 128], F16))
            dt16 = es_in.enter_context(_sbuf(nc, "dt16", [32, 2048], F16))
            abuf = es_in.enter_context(_sbuf(nc, "abuf", [128, 2048], F16))
            sz16 = es_in.enter_context(_sbuf(nc, "sz16", [128, 512], F16))
            with _psum(nc, "ps_m", [128, 4, 512], F32) as ps:
                ntok_xs = 2048 + hoff
                for ch in range(CH):
                    buf = ch % 2
                    p.wait("gpsimd", "dv", B.wfree.get(("wi", buf), 0))
                    gwv = _g_dma(p, wib[:, buf, :, :],
                                 winT[:, :, ch * 128:(ch + 1) * 128],
                                 inc="wi%d" % buf)
                    p.wait("tensor", "wi%d" % buf, gwv)
                    if hf == 1:
                        p.wait("vector", "ac", p.cnt["ac"])
                        _cp(p, xs[:, 0:3], B.xs_keep[:, ch, :], inc="dv")
                    nt = 0
                    while nt < ntok_xs:
                        w = min(512, ntok_xs - nt)
                        slot = (nt // 512) % 4
                        p.wait("tensor", "dv", p.cnt["dv"])
                        p.wait("tensor", "ac", p.cnt["ac"])
                        for k in range(KC):
                            _mm(p, ps[:, slot, 0:w], wib[:, buf, k, :],
                                xn[:, k, nt:nt + w], k == 0, k == KC - 1,
                                inc="mm" if k == KC - 1 else None)
                        p.wait("vector", "mm", p.cnt["mm"])
                        dvv = _cp(p, xs[:, (3 - hoff) + nt:(3 - hoff) + nt + w],
                                  ps[:, slot, 0:w], inc="dv")
                        B.wfree[("wi", buf)] = dvv
                        nt += w
                    if hf == 0:
                        _cp(p, B.xs_keep[:, ch, :], xs[:, 2048:2051], inc="dv")
                    if KDBG and hf == 0 and ch == 0:
                        p.wait("sync", "dv", p.cnt["dv"])
                        _s_dma(p, B.d["dbg_xs"][:, :], xs[:, :], inc="dbq")
                        p.wait("sync", "dbq", p.cnt["dbq"])
                        p.bar()
                    for t in range(4):
                        tsl = slice(t * 512, (t + 1) * 512)
                        p.wait("vector", "dv", p.cnt["dv"])
                        p.wait("vector", "ac", p.cnt["ac"])
                        _ts(p, B.tmpf[:, 0:512], xs[:, t * 512:t * 512 + 512],
                            B.convwc[:, ch, 0:1], None, ALU.mult, ALU.bypass,
                            inc="dv"); p.vs()
                        for j in range(1, 4):
                            _stt(p, B.tmpf[:, 0:512],
                                 xs[:, t * 512 + j:t * 512 + j + 512],
                                 B.convwc[:, ch, j:j + 1], B.tmpf[:, 0:512],
                                 ALU.mult, ALU.add, inc="dv"); p.vs()
                        p.wait("scalar", "dv", p.cnt["dv"])
                        _act(p, xc[:, ch, tsl], B.tmpf[:, 0:512], AF.Silu,
                             bias=B.convbc[:, ch:ch + 1], inc="ac")
                p.wait("tensor", "ac", p.cnt["ac"])
                if KDBG and hf == 0:
                    p.wait("sync", "ac", p.cnt["ac"])
                    for ch in range(CH):
                        _s_dma(p, B.d["dbg_xc"][:, ch, :], xc[:, ch, :], inc="dbq")
                    p.wait("sync", "dbq", p.cnt["dbq"])
                    p.bar()
                for t in range(4):
                    tsl = slice(t * 512, (t + 1) * 512)
                    slot = t % 2
                    p.wait("tensor", "dv", p.cnt["dv"])
                    for k in range(CH):
                        _mm(p, ps[0:32, slot, :], B.wxT[:, k, :], xc[:, k, tsl],
                            k == 0, k == CH - 1, inc="mm" if k == CH - 1 else None)
                    p.wait("vector", "mm", p.cnt["mm"])
                    _cp(p, dt16[:, tsl], ps[0:32, slot, :], inc="dv")
                if KDBG and hf == 0:
                    p.wait("sync", "dv", p.cnt["dv"])
                    _s_dma(p, B.d["dbg_dt"][:, :], dt16[:, :], inc="dbq")
                    p.wait("sync", "dbq", p.cnt["dbq"])
                p.bar()
                for ch in range(CH):
                    buf = ch % 2
                    for t in range(4):
                        tsl = slice(t * 512, (t + 1) * 512)
                        slot = t % 2
                        p.wait("tensor", "ac", p.cnt["ac"])
                        _mm(p, ps[:, slot, :], B.wdtT[:, ch * 128:(ch + 1) * 128],
                            dt16[:, tsl], True, True, inc="mm")
                        p.wait("scalar", "mm", p.cnt["mm"])
                        p.wait("scalar", "dv", p.cnt["dv"])
                        _act(p, B.e1[:, 0:512], ps[:, slot, :], AF.Abs,
                             bias=B.bdtc[:, ch:ch + 1])
                        _act(p, B.e1[:, 0:512], B.e1[:, 0:512], AF.Exp, scale=-1.0)
                        _act(p, B.e1[:, 0:512], B.e1[:, 0:512], AF.Ln, bias=B.c_one[:, 0:1])
                        _act(p, B.e2[:, 0:512], ps[:, slot, :], AF.Relu,
                             bias=B.bdtc[:, ch:ch + 1], inc="ac")
                        p.wait("vector", "ac", p.cnt["ac"])
                        _tt(p, abuf[:, tsl], B.e1[:, 0:512], B.e2[:, 0:512], ALU.add,
                            inc="dv")
                    if KDBG and hf == 0 and ch == 0:
                        p.wait("sync", "dv", p.cnt["dv"])
                        _s_dma(p, B.d["dbg_ab"][:, :], abuf[:, :], inc="dbq")
                        p.wait("sync", "dbq", p.cnt["dbq"])
                        p.bar()
                    p.wait("gpsimd", "dv", B.wfree.get(("wi", buf), 0))
                    gwv = _g_dma(p, wib[:, buf, :, :],
                                 winT[:, :, 2048 + ch * 128:2048 + (ch + 1) * 128],
                                 inc="wi%d" % buf)
                    p.wait("tensor", "wi%d" % buf, gwv)
                    for t in range(4):
                        tsl = slice(t * 512, (t + 1) * 512)
                        slot = 2 + t % 2
                        p.wait("tensor", "ac", p.cnt["ac"])
                        p.wait("tensor", "dv", p.cnt["dv"])
                        for k in range(KC):
                            _mm(p, ps[:, slot, :], wib[:, buf, k, :],
                                xn[:, k, hoff + t * 512:hoff + t * 512 + 512],
                                k == 0, k == KC - 1, inc="mm" if k == KC - 1 else None)
                        p.wait("scalar", "mm", p.cnt["mm"])
                        _act(p, sz16[:, :], ps[:, slot, :], AF.Silu, inc="ac")
                        p.wait("vector", "ac", p.cnt["ac"])
                        _tt(p, B.tmpf[:, 0:512], abuf[:, tsl], xc[:, ch, tsl], ALU.mult)
                        dvv = _tt(p, xc[:, ch, tsl], B.tmpf[:, 0:512], sz16[:, :],
                                  ALU.mult, inc="dv")
                        B.wfree[("wi", buf)] = dvv
                p.bar()
                es_in.close()
                with _sbuf(nc, "wob", [128, 2, CH, 128], F16) as wob:
                    for m in range(8):
                        buf = m % 2
                        p.wait("gpsimd", "dv", B.wfree.get(("wo2", buf), 0))
                        gwv = _g_dma(p, wob[:, buf, :, :],
                                     woutT[:, :, m * 128:(m + 1) * 128],
                                     inc="wo2%d" % buf)
                        p.wait("tensor", "wo2%d" % buf, gwv)
                        for t in range(4):
                            tsl = slice(t * 512, (t + 1) * 512)
                            slot = t % 4
                            p.wait("tensor", "dv", p.cnt["dv"])
                            for k in range(CH):
                                _mm(p, ps[:, slot, :], wob[:, buf, k, :], xc[:, k, tsl],
                                    k == 0, k == CH - 1,
                                    inc="mm" if k == CH - 1 else None)
                            p.wait("vector", "mm", p.cnt["mm"])
                            xsl = B.xres[:, m, h0 + t * 512:h0 + (t + 1) * 512]
                            _stt(p, B.tmpf[:, 0:512], ps[:, slot, :], 1.0,
                                 xn[:, m, hoff + t * 512:hoff + t * 512 + 512],
                                 ALU.mult, ALU.add)
                            dvv = _stt(p, xsl, B.tmpf[:, 0:512], IRS, xsl,
                                       ALU.mult, ALU.add, inc="dv")
                            B.wfree[("wo2", buf)] = dvv
                    p.wait("tensor", "dv", p.cnt["dv"])
        p.bar()


def emit_store(p, B):
    nc = p.nc
    with _sbuf(nc, "stage", [128, 2, 512], F32) as stage:
        xo = B.d["xo"].rearrange("(k p) n -> p k n", p=128)
        for k in range(KC):
            for t in range(8):
                i = k * 8 + t
                slot = i % 2
                if i >= 2:
                    p.wait("vector", "st%d" % slot, p.cnt["st%d" % slot])
                _ts(p, stage[:, slot, :], B.xres[:, k, t * 512:(t + 1) * 512],
                    RS, None, ALU.mult, ALU.bypass, inc="dv")
                p.wait("sync", "dv", p.cnt["dv"])
                _s_dma(p, xo[:, k, t * 512:(t + 1) * 512], stage[:, slot, :],
                       inc="st%d" % slot)
        p.wait("sync", "st0", p.cnt["st0"])
        p.wait("sync", "st1", p.cnt["st1"])
        p.wait("gpsimd", "st0", p.cnt["st0"])
        p.wait("gpsimd", "st1", p.cnt["st1"])




class Bag:
    pass


def build_program(blocks):
    nc = bass.Bass("TRN2", target_bir_lowering=False)
    B = Bag()
    B.wfree = {}
    B.wbfree = {}
    d = {}
    d16 = {}
    need = set(blocks)

    def d16t(name, shape):
        d16[name] = nc.dram_tensor(name + "_f16", list(shape), F16,
                                   kind="Internal").ap()

    def din(name, shape):
        d[name] = nc.dram_tensor(name, list(shape), F32, kind="ExternalInput").ap()

    din("x", (DM, TOK))
    din("ident", (128, 128))
    din("sel8", (8, 1024))
    din("bsel", (1, 4))
    din("hsel", (1, 8))
    if 0 in need:
        for n in ("wqT", "wkT", "wvT", "woT"):
            din(n, (DM, DM))
        for n in ("bq", "bk", "bv", "bo"):
            din(n, (128, 8))
    if 1 in need:
        din("gwT", (128, 8, 8))
        din("l0w1", (8, DM, 4096)); din("l0b1", (128, 32, 8))
        din("l0w2", (8, 4096, DM)); din("l0b2", (128, 8, 8))
        din("l0sw1", (DM, 4096)); din("l0sb1", (128, 32))
        din("l0sw2", (4096, DM)); din("l0sb2", (128, 8))
        d16t("l0w1", (8, DM, 4096)); d16t("l0w2", (8, 4096, DM))
        d16t("l0sw1", (DM, 4096)); d16t("l0sw2", (4096, DM))
    if 2 in need:
        din("winT", (DM, 4096))
        din("convw", (128, 16, 4)); din("convb", (128, 16))
        din("wxT", (128, 16, 32)); din("wdtT", (32, 2048)); din("bdt", (128, 16))
        din("woutT", (2048, DM))
    if 3 in need:
        din("wgT", (128, 8, 8))
        din("w1s", (DM, 4096)); din("w2s", (DM, 4096)); din("w3s", (4096, DM))
        din("l1w1", (8, DM, 4096)); din("l1w2", (8, DM, 4096))
        din("l1w3", (8, 4096, DM))
        d16t("w1s", (DM, 4096)); d16t("w2s", (DM, 4096)); d16t("w3s", (4096, DM))
        d16t("l1w1", (8, DM, 4096)); d16t("l1w2", (8, DM, 4096))
        d16t("l1w3", (8, 4096, DM))
    d["xo"] = nc.dram_tensor("xo", [DM, TOK], F32, kind="ExternalOutput").ap()
    if KDBG:
        d["dbg_xn"] = nc.dram_tensor("dbg_xn", [DM, TOK], F16, kind="ExternalOutput").ap()
        d["dbg_kv"] = nc.dram_tensor("dbg_kv", [128, 512], F32, kind="ExternalOutput").ap()
        d["dbg_kv16"] = nc.dram_tensor("dbg_kv16", [128, 8, 64], F16, kind="ExternalOutput").ap()
        d["dbg_q"] = nc.dram_tensor("dbg_q", [128, 8, 512], F16, kind="ExternalOutput").ap()
        d["dbg_attn"] = nc.dram_tensor("dbg_attn", [128, 8, 512], F16, kind="ExternalOutput").ap()
        d["dbg_lg"] = nc.dram_tensor("dbg_lg", [8, 1024], F32, kind="ExternalOutput").ap()
        d["dbg_cfm"] = nc.dram_tensor("dbg_cfm", [8, 1024], F16, kind="ExternalOutput").ap()
        d["dbg_h"] = nc.dram_tensor("dbg_h", [128, 32, 1024], F16, kind="ExternalOutput").ap()
        d["dbg_ltok"] = nc.dram_tensor("dbg_ltok", [128, 8, 8], F32, kind="ExternalOutput").ap()
        d["dbg_cmb"] = nc.dram_tensor("dbg_cmb", [128, 8, 8], F32, kind="ExternalOutput").ap()
        d["dbg_xnh"] = nc.dram_tensor("dbg_xnh", [128, 8, 2051], F16, kind="ExternalOutput").ap()
        d["dbg_xs"] = nc.dram_tensor("dbg_xs", [128, 2051], F16, kind="ExternalOutput").ap()
        d["dbg_xc"] = nc.dram_tensor("dbg_xc", [128, 16, 2048], F16, kind="ExternalOutput").ap()
        d["dbg_dt"] = nc.dram_tensor("dbg_dt", [32, 2048], F16, kind="ExternalOutput").ap()
        d["dbg_ab"] = nc.dram_tensor("dbg_ab", [128, 2048], F16, kind="ExternalOutput").ap()
        d["dbg_h16"] = nc.dram_tensor("dbg_h16", [128, 8, 128], F16, kind="ExternalOutput").ap()
        d["dbg_hsel"] = nc.dram_tensor("dbg_hsel", [128, 8], F32, kind="ExternalOutput").ap()
        d["dbg_srow"] = nc.dram_tensor("dbg_srow", [1, 3, 512], F32, kind="ExternalOutput").ap()
    B.d = d
    B.d16 = d16
    if 0 in need:
        B.kv_in_d = nc.dram_tensor("kv_in_d", [128, 2048], F32, kind="Internal").ap()
        B.kv_out_d = nc.dram_tensor("kv_out_d", [128, 2048], F32, kind="Internal").ap()
    if 2 in need:
        B.halo_in_d = nc.dram_tensor("halo_in_d", [DM, 3], F32, kind="Internal").ap()
        B.halo_out_d = nc.dram_tensor("halo_out_d", [8 * DM, 3], F32,
                                      kind="Internal").ap()

    with contextlib.ExitStack() as es:
        p = Plan(nc, es)
        B.xres = es.enter_context(_sbuf(nc, "xres", [128, KC, TOK], F16))
        B.srow = es.enter_context(_sbuf(nc, "srow", [1, 3, 512], F32))
        B.row16 = es.enter_context(_sbuf(nc, "row16", [1, 2, 512], F16))
        B.tmpf = es.enter_context(_sbuf(nc, "tmpf", [128, 512], F32))
        B.e1 = es.enter_context(_sbuf(nc, "e1", [128, 512], F16))
        B.e2 = es.enter_context(_sbuf(nc, "e2", [128, 512], F16))
        B.ones_col = es.enter_context(_sbuf(nc, "ones_col", [128, 1], F16))
        B.c_eps = es.enter_context(_sbuf(nc, "c_eps", [1, 1], F32))
        B.c_one = es.enter_context(_sbuf(nc, "c_one", [128, 1], F32))
        B.ones_row = es.enter_context(_sbuf(nc, "ones_row", [1, 128], F16))
        B.ident = es.enter_context(_sbuf(nc, "identS", [128, 128], F16))
        B.sel8 = es.enter_context(_sbuf(nc, "sel8S", [8, 1024], F16))
        B.bselr = es.enter_context(_sbuf(nc, "bselr", [1, 4], F32))
        B.hselr = es.enter_context(_sbuf(nc, "hselr", [1, 8], F32))
        B.bsel_bc = es.enter_context(_sbuf(nc, "bsel_bc", [128, 4], F32))
        B.hsel_bc = es.enter_context(_sbuf(nc, "hsel_bc", [128, 8], F32))
        if 0 in need:
            for n in ("bqc", "bkc", "bvc", "boc", "negbq", "negbk", "boS"):
                setattr(B, n, es.enter_context(_sbuf(nc, n, [128, 8], F32)))
        if 1 in need:
            B.gwT = es.enter_context(_sbuf(nc, "gwTs", [128, 8, 8], F16))
            B.l0b1c = es.enter_context(_sbuf(nc, "l0b1c", [128, 32, 8], F32))
            B.l0b2c = es.enter_context(_sbuf(nc, "l0b2c", [128, 8, 8], F32))
            B.l0sb1c = es.enter_context(_sbuf(nc, "l0sb1c", [128, 32], F32))
            B.l0sb2c = es.enter_context(_sbuf(nc, "l0sb2c", [128, 8], F32))
            B.l0sb2S = es.enter_context(_sbuf(nc, "l0sb2S", [128, 8], F32))
        if 2 in need:
            B.convwc = es.enter_context(_sbuf(nc, "convwc", [128, 16, 4], F32))
            B.convbc = es.enter_context(_sbuf(nc, "convbc", [128, 16], F32))
            B.bdtc = es.enter_context(_sbuf(nc, "bdtc", [128, 16], F32))
            B.wxT = es.enter_context(_sbuf(nc, "wxTs", [128, 16, 32], F16))
            B.wdtT = es.enter_context(_sbuf(nc, "wdtTs", [32, 2048], F16))
            B.halo16 = es.enter_context(_sbuf(nc, "halo16", [128, KC, 128], F16))
            B.xs_keep = es.enter_context(_sbuf(nc, "xs_keep", [128, 16, 3], F16))
        if 3 in need:
            B.wgT = es.enter_context(_sbuf(nc, "wgTs", [128, 8, 8], F16))

        with nc.Block() as blk:
            emit_init(p, B, need)
            for b in blocks:
                if b == 0:
                    emit_l0_attn(p, B)
                elif b == 1:
                    emit_moe(p, B, "l0")
                elif b == 2:
                    emit_l1_attn(p, B)
                elif b == 3:
                    emit_moe(p, B, "l1")
            emit_store(p, B)

            @blk.sync
            def _(E):
                for f in p.streams["sync"]:
                    f(E)

            @blk.tensor
            def _(E):
                for f in p.streams["tensor"]:
                    f(E)

            @blk.vector
            def _(E):
                for f in p.streams["vector"]:
                    f(E)

            @blk.scalar
            def _(E):
                for f in p.streams["scalar"]:
                    f(E)

            @blk.gpsimd
            def _(E):
                for f in p.streams["gpsimd"]:
                    f(E)
    return nc


def _prep_weights(I):
    g = {}
    T = lambda a: np.ascontiguousarray(np.asarray(a).T.astype(np.float32))
    C = lambda a: np.ascontiguousarray(np.asarray(a, dtype=np.float32))
    g["wqT"] = T(I["L0_attn_Wq"]); g["wkT"] = T(I["L0_attn_Wk"])
    g["wvT"] = T(I["L0_attn_Wv"]); g["woT"] = T(I["L0_attn_Wo"])
    for n, src in (("bq", "L0_attn_bq"), ("bk", "L0_attn_bk"),
                   ("bv", "L0_attn_bv"), ("bo", "L0_attn_bo")):
        g[n] = C(np.asarray(I[src]).reshape(8, 128).T)
    g["gwT"] = C(np.asarray(I["L0_ffn_gate_w"]).T.reshape(8, 128, 8).transpose(1, 0, 2))
    g["l0w1"] = C(I["L0_ffn_W1"])
    g["l0b1"] = C(np.asarray(I["L0_ffn_b1"]).reshape(8, 32, 128).transpose(2, 1, 0))
    g["l0w2"] = C(I["L0_ffn_W2"])
    g["l0b2"] = C(np.asarray(I["L0_ffn_b2"]).reshape(8, 8, 128).transpose(2, 1, 0))
    g["l0sw1"] = C(I["L0_ffn_sW1"])
    g["l0sb1"] = C(np.asarray(I["L0_ffn_sb1"]).reshape(32, 128).T)
    g["l0sw2"] = C(I["L0_ffn_sW2"])
    g["l0sb2"] = C(np.asarray(I["L0_ffn_sb2"]).reshape(8, 128).T)
    g["winT"] = T(I["L1_attn_W_in"])
    g["convw"] = C(np.asarray(I["L1_attn_conv_w"])[:, 0, :].reshape(16, 128, 4)
                   .transpose(1, 0, 2))
    g["convb"] = C(np.asarray(I["L1_attn_conv_b"]).reshape(16, 128).T)
    g["wxT"] = C(np.asarray(I["L1_attn_W_x"])[0:32, :].T.reshape(16, 128, 32)
                 .transpose(1, 0, 2))
    g["wdtT"] = T(I["L1_attn_W_dt"])
    g["bdt"] = C(np.asarray(I["L1_attn_b_dt"]).reshape(16, 128).T)
    g["woutT"] = T(I["L1_attn_W_out"])
    g["w1s"] = C(I["L1_ffn_w1s"]); g["w2s"] = C(I["L1_ffn_w2s"])
    g["w3s"] = C(I["L1_ffn_w3s"])
    g["l1w1"] = C(I["L1_ffn_W1"]); g["l1w2"] = C(I["L1_ffn_W2"])
    g["l1w3"] = C(I["L1_ffn_W3"])
    g["wgT"] = C(np.asarray(I["L1_ffn_Wg"]).reshape(8, 128, 8).transpose(1, 0, 2))
    g["ident"] = np.eye(128, dtype=np.float32)
    sel8 = np.zeros((8, 1024), dtype=np.float32)
    for e in range(8):
        sel8[e, e * 128:(e + 1) * 128] = 1.0
    g["sel8"] = sel8
    return g


_INPUT_KEYS = {
    0: ["wqT", "wkT", "wvT", "woT", "bq", "bk", "bv", "bo"],
    1: ["gwT", "l0w1", "l0b1", "l0w2", "l0b2", "l0sw1", "l0sb1", "l0sw2", "l0sb2"],
    2: ["winT", "convw", "convb", "wxT", "wdtT", "bdt", "woutT"],
    3: ["wgT", "w1s", "w2s", "w3s", "l1w1", "l1w2", "l1w3"],
}


def _run_blocks(inputs, launch_groups):
    g = _prep_weights(inputs)
    x = np.asarray(inputs["L0_attn_x"], dtype=np.float32)
    shards = []
    for c in range(NCORES):
        b, h = c // 2, c % 2
        shards.append(np.ascontiguousarray(x[b, h * TOK:(h + 1) * TOK, :].T))
    for grp in launch_groups:
        nc = build_program(grp)
        keys = ["ident", "sel8"]
        for b in grp:
            keys += _INPUT_KEYS[b]
        in_maps = []
        for c in range(NCORES):
            bsel = np.zeros((1, 4), dtype=np.float32); bsel[0, c // 2] = 1.0
            hsel = np.zeros((1, 8), dtype=np.float32)
            if c % 2 == 1:
                hsel[0, c - 1] = 1.0
            m = {k: g[k] for k in keys}
            m["x"] = shards[c]
            m["bsel"] = bsel
            m["hsel"] = hsel
            in_maps.append(m)
        res = run_bass_kernel_spmd(nc, in_maps, core_ids=list(range(NCORES)))
        shards = [np.ascontiguousarray(r["xo"]) for r in res.results]
    out = np.empty((4, 8192, DM), dtype=np.float32)
    for c in range(NCORES):
        b, h = c // 2, c % 2
        out[b, h * TOK:(h + 1) * TOK, :] = shards[c].T
    return out


def kernel(**inputs):
    return _run_blocks(inputs, LAUNCH_GROUPS)


# version 05 (final)
# speedup vs baseline: 1.0558x; 1.0558x over previous
import contextlib
import os
import numpy as np

import concourse.bass as bass
import concourse.mybir as mybir
from concourse.bass_utils import run_bass_kernel_spmd

F32 = mybir.dt.float32
F16 = mybir.dt.float16
BF16 = mybir.dt.bfloat16
AF = mybir.ActivationFunctionType
ALU = mybir.AluOpType
AX = mybir.AxisListType

NCORES = 8
TOK = 4096
DM = 1024
KC = 8
RG = [list(range(NCORES))]
ENGS = ("sync", "tensor", "vector", "scalar", "gpsimd")
RS = 256.0
IRS = 1.0 / RS

LAUNCH_GROUPS = [[0, 1, 2, 3]]
KDBG = int(os.environ.get("KDBG", "0"))


class Plan:
    def __init__(self, nc, es):
        self.nc = nc
        self.es = es
        self.streams = {e: [] for e in ENGS}
        self.sems = {}
        self.cnt = {}

    def sem(self, name):
        if name not in self.sems:
            self.sems[name] = self.es.enter_context(self.nc.semaphore(name))
            self.cnt[name] = 0
        return self.sems[name]

    def plan(self, eng, builder, inc=None, d=1):
        v = None
        h = None
        if inc is not None:
            h = self.sem(inc)
            self.cnt[inc] += d
            v = self.cnt[inc]

        def f(E, b=builder, h=h, d=d):
            r = b(E)
            if h is not None:
                r.then_inc(h, d)
        self.streams[eng].append(f)
        return v

    def wait(self, eng, sname, val):
        h = self.sem(sname)
        self.streams[eng].append(lambda E, h=h, v=val: E.wait_ge(h, v))

    def sem_inc(self, eng, sname, d=1):
        h = self.sem(sname)
        self.cnt[sname] += d
        v = self.cnt[sname]
        self.streams[eng].append(lambda E, h=h, d=d: E.sem_inc(h, d))
        return v

    def vs(self):
        self.wait("vector", "dv", self.cnt["dv"])

    def bar(self, engs=ENGS):
        for e, sn in (("vector", "dv"), ("scalar", "ac")):
            if e in engs and sn in self.cnt:
                self.wait(e, sn, self.cnt[sn])
        vals = {}
        for e in engs:
            vals[e] = self.sem_inc(e, "bar_" + e)
        for e in engs:
            for o in engs:
                if o != e:
                    self.wait(e, "bar_" + o, vals[o])

    def cbar(self):
        self.bar(("tensor", "vector", "scalar"))



_UID = [0]


def _sbuf(nc, name, shape, dtype):
    _UID[0] += 1
    return nc.sbuf_tensor("%s_u%d" % (name, _UID[0]), shape, dtype)


def _psum(nc, name, shape, dtype):
    _UID[0] += 1
    return nc.psum_tensor("%s_u%d" % (name, _UID[0]), shape, dtype)


def _mm(p, out, lhsT, rhs, start, stop, inc=None, skip=False):
    return p.plan("tensor",
                  lambda E, o=out, l=lhsT, r=rhs, s1=start, s2=stop, sk=skip:
                  E.matmul(o, l, r, start=s1, stop=s2, skip_group_check=sk), inc=inc)


def _g_dma(p, out, in_, inc="gw"):
    return p.plan("gpsimd", lambda E, o=out, i=in_: E.dma_start(out=o, in_=i),
                  inc=inc, d=16)


def _s_dma(p, out, in_, inc="sx"):
    return p.plan("sync", lambda E, o=out, i=in_: E.dma_start(out=o, in_=i),
                  inc=inc, d=16)


def _act(p, out, in_, func, bias=None, scale=None, inc=None):
    kw = {}
    if bias is not None:
        kw["bias"] = bias
    if scale is not None:
        kw["scale"] = scale
    return p.plan("scalar",
                  lambda E, o=out, i=in_, f=func, kw=dict(kw): E.activation(o, i, f, **kw),
                  inc=inc)


def _tt(p, out, a, b, op, inc=None):
    return p.plan("vector",
                  lambda E, o=out, x=a, y=b, op=op: E.tensor_tensor(o, x, y, op), inc=inc)


def _ts(p, out, a, s1, s2, op0, op1, inc=None):
    return p.plan("vector",
                  lambda E, o=out, x=a, s1=s1, s2=s2, o0=op0, o1=op1:
                  E.tensor_scalar(o, x, s1, s2, o0, o1), inc=inc)


def _stt(p, out, in0, scalar, in1, op0, op1, inc=None):
    return p.plan("vector",
                  lambda E, o=out, a=in0, s=scalar, b=in1, o0=op0, o1=op1:
                  E.scalar_tensor_tensor(o, a, s, b, o0, o1), inc=inc)


def _cp(p, out, in_, inc=None):
    return p.plan("vector", lambda E, o=out, i=in_: E.tensor_copy(o, i), inc=inc)


def _red(p, out, in_, op, inc=None):
    return p.plan("vector",
                  lambda E, o=out, i=in_, op=op: E.tensor_reduce(o, i, AX.X, op), inc=inc)


def _tr(p, out, in_, ident, inc=None):
    return p.plan("tensor",
                  lambda E, o=out, i=in_, d=ident: E.transpose(o, i, d), inc=inc)




def emit_init(p, B, need):
    nc = p.nc
    p.plan("vector", lambda E, t=B.ones_col: E.memset(t[:, :], 1.0))
    p.plan("vector", lambda E, t=B.c_eps: E.memset(t[:, :], 1e-5 / (RS * RS)))
    p.plan("vector", lambda E, t=B.c_one: E.memset(t[:, :], 1.0))
    p.plan("vector", lambda E, t=B.ones_row: E.memset(t[:, :], 1.0), inc="dv")
    _g_dma(p, B.ident[:, :], B.d["ident"])
    _g_dma(p, B.sel8[:, :], B.d["sel8"])
    xr = B.d["x"].rearrange("(k p) n -> p k n", p=128)
    small = [(B.bselr, "bsel"), (B.hselr, "hsel")]
    if 0 in need:
        small += [(B.bqc, "bq"), (B.bkc, "bk"), (B.bvc, "bv"), (B.boc, "bo")]
    if 1 in need:
        small += [(B.l0b1c, "l0b1"), (B.l0b2c, "l0b2"), (B.l0sb1c, "l0sb1"),
                  (B.l0sb2c, "l0sb2")]
        _g_dma(p, B.gwT[:, :, :], B.d["gwT"])
    if 2 in need:
        small += [(B.convwc, "convw"), (B.convbc, "convb"), (B.bdtc, "bdt")]
        _g_dma(p, B.wxT[:, :, :], B.d["wxT"])
        _g_dma(p, B.wdtT[:, :], B.d["wdtT"])
    if 3 in need:
        _g_dma(p, B.wgT[:, :, :], B.d["wgT"])
    for t, d_name in small:
        _s_dma(p, t.ap(), B.d[d_name])
    p.wait("gpsimd", "gw", p.cnt["gw"])
    p.wait("sync", "sx", p.cnt["sx"])
    p.bar()
    with _psum(nc, "ps_init", [128, 16], F32) as ps_init:
        _cp(p, B.row16[0:1, 0, 0:4], B.bselr[0:1, :])
        _cp(p, B.row16[0:1, 0, 4:12], B.hselr[0:1, :], inc="dv")
        p.wait("tensor", "dv", p.cnt["dv"])
        _mm(p, ps_init[:, 0:12], B.ones_row[0:1, :], B.row16[0:1, 0, 0:12], True, True,
            inc="mm")
        p.wait("vector", "mm", p.cnt["mm"])
        _cp(p, B.bsel_bc[:, :], ps_init[:, 0:4])
        _cp(p, B.hsel_bc[:, :], ps_init[:, 4:12], inc="dv")
        if 0 in need:
            _ts(p, B.negbq[:, :], B.bqc[:, :], -1.0, None, ALU.mult, ALU.bypass)
            _ts(p, B.negbk[:, :], B.bkc[:, :], -1.0, None, ALU.mult, ALU.bypass)
            _ts(p, B.boS[:, :], B.boc[:, :], IRS, None, ALU.mult, ALU.bypass, inc="dv")
        if 1 in need:
            _ts(p, B.l0sb2S[:, :], B.l0sb2c[:, :], IRS, None, ALU.mult, ALU.bypass,
                inc="dv")
        p.wait("tensor", "dv", p.cnt["dv"])
    with _sbuf(nc, "xstage", [128, 4, 512], F32) as xstage:
        for k in range(KC):
            for t in range(8):
                i = k * 8 + t
                slot = i % 4
                if i >= 4:
                    p.wait("sync", "dv", p.cnt["dv"] - 3)
                sv = _s_dma(p, xstage[:, slot, :], xr[:, k, t * 512:(t + 1) * 512],
                            inc="lx%d" % slot)
                p.wait("vector", "lx%d" % slot, sv)
                _ts(p, B.xres[:, k, t * 512:(t + 1) * 512], xstage[:, slot, :],
                    IRS, None, ALU.mult, ALU.bypass, inc="dv")
        p.wait("sync", "dv", p.cnt["dv"])
    p.bar()


def emit_ln(p, B, src, src_off, tok_cnt, xn, xn_off, pad=0, cb=False):
    nc = p.nc
    tiles = []
    t0 = 0
    while t0 < tok_cnt:
        w = min(512, tok_cnt - t0)
        tiles.append((t0, max(w, pad)))
        t0 += w
    with _psum(nc, "ln_ps", [128, 4, 512], F32) as ps, \
            _sbuf(nc, "ln_sq", [128, KC, 512], BF16) as sq:
        for (t0, w) in tiles:
            sl = slice(src_off + t0, src_off + t0 + w)
            so = slice(xn_off + t0, xn_off + t0 + w)
            p.wait("scalar", "mm", p.cnt["mm"])
            for k in range(KC):
                _act(p, sq[:, k, 0:w], src[:, k, sl], AF.Square,
                     inc="ac" if k == KC - 1 else None)
            p.wait("tensor", "ac", p.cnt["ac"])
            p.wait("tensor", "dv", p.cnt["dv"])
            for k in range(KC):
                _mm(p, ps[0:1, 0, 0:w], B.ones_col[:, 0:1], src[:, k, sl],
                    k == 0, k == KC - 1)
            for k in range(KC):
                _mm(p, ps[0:1, 1, 0:w], B.ones_col[:, 0:1], sq[:, k, 0:w],
                    k == 0, k == KC - 1, inc="mm" if k == KC - 1 else None)
            p.wait("vector", "mm", p.cnt["mm"])
            _ts(p, B.srow[0:1, 0, 0:w], ps[0:1, 0, 0:w], 1.0 / DM, None,
                ALU.mult, ALU.bypass, inc="dv")
            _ts(p, B.srow[0:1, 1, 0:w], ps[0:1, 1, 0:w], 1.0 / DM, None,
                ALU.mult, ALU.bypass, inc="dv"); p.vs()
            _tt(p, B.srow[0:1, 2, 0:w], B.srow[0:1, 0, 0:w], B.srow[0:1, 0, 0:w],
                ALU.mult, inc="dv"); p.vs()
            _tt(p, B.srow[0:1, 1, 0:w], B.srow[0:1, 1, 0:w], B.srow[0:1, 2, 0:w],
                ALU.subtract, inc="dv"); p.vs()
            p.wait("scalar", "dv", p.cnt["dv"])
            _act(p, B.srow[0:1, 1, 0:w], B.srow[0:1, 1, 0:w], AF.Sqrt, bias=B.c_eps[0:1, 0:1],
                 inc="ac")
            p.wait("vector", "ac", p.cnt["ac"])
            p.plan("vector", lambda E, o=B.srow[0:1, 2, 0:w], i=B.srow[0:1, 1, 0:w]:
                   E.reciprocal(o, i), inc="dv")
            p.vs()
            _ts(p, B.srow[0:1, 2, 0:w], B.srow[0:1, 2, 0:w], 60000.0, None,
                ALU.min, ALU.bypass, inc="dv"); p.vs()
            _cp(p, B.row16[0:1, 0, 0:w], B.srow[0:1, 0, 0:w])
            _cp(p, B.row16[0:1, 1, 0:w], B.srow[0:1, 2, 0:w], inc="dv")
            p.wait("tensor", "dv", p.cnt["dv"])
            _mm(p, ps[:, 2, 0:w], B.ones_row[0:1, :], B.row16[0:1, 0, 0:w], True, True)
            _mm(p, ps[:, 3, 0:w], B.ones_row[0:1, :], B.row16[0:1, 1, 0:w], True, True,
                inc="mm")
            p.wait("vector", "mm", p.cnt["mm"])
            for k in range(KC):
                _tt(p, B.tmpf[:, 0:w], src[:, k, sl], ps[:, 2, 0:w], ALU.subtract)
                _tt(p, xn[:, k, so], B.tmpf[:, 0:w], ps[:, 3, 0:w], ALU.mult,
                    inc="dv" if k == KC - 1 else None)
        p.wait("tensor", "dv", p.cnt["dv"])
    if cb:
        p.cbar()
    else:
        p.bar()


def _elu1_evac(p, B, ps_tile, negb_col, posb_col, out16, w, par=0):
    g = B.egate.get(par)
    if g:
        p.wait("scalar", "dv", g)
    e1 = B.e1[:, par, 0:w]
    e2 = B.e2[:, par, 0:w]
    _act(p, e1, ps_tile, AF.Relu, bias=negb_col, scale=-1.0)
    _act(p, e1, e1, AF.Exp, scale=-1.0)
    _act(p, e2, ps_tile, AF.Relu, bias=posb_col, inc="ac")
    p.wait("vector", "ac", p.cnt["ac"])
    v = _tt(p, out16, e1, e2, ALU.add, inc="dv")
    B.egate[par] = v
    return v


def emit_l0_attn(p, B):
    nc = p.nc
    with contextlib.ExitStack() as es:
        xn = es.enter_context(_sbuf(nc, "xn_full", [128, KC, TOK], F16))
        emit_ln(p, B, B.xres, 0, TOK, xn, 0)
        if KDBG:
            dxn = B.d["dbg_xn"].rearrange("(k p) n -> p k n", p=128)
            for k in range(KC):
                _s_dma(p, dxn[:, k, :], xn[:, k, :], inc="dbq")
            p.wait("sync", "dbq", p.cnt["dbq"])
            p.bar()

        wa = es.enter_context(_sbuf(nc, "wa", [128, 2, KC, 512], F16))
        k_sb = es.enter_context(_sbuf(nc, "k_sb", [128, 4, 512], F16))
        v_sb = es.enter_context(_sbuf(nc, "v_sb", [128, 4, 512], F16))
        ktok = es.enter_context(_sbuf(nc, "ktok", [128, 4, 4, 128], F16))
        vtok = es.enter_context(_sbuf(nc, "vtok", [128, 4, 4, 128], F16))
        kv16 = es.enter_context(_sbuf(nc, "kv16", [128, 8, 64], F16))
        ps_r = es.enter_context(_psum(nc, "ps_r", [128, 6, 512], F32))
        ps_kv = es.enter_context(_psum(nc, "ps_kv", [128, 512], F32))
        psT = es.enter_context(_psum(nc, "psT", [128, 4, 128], F16))

        wkT = B.d["wkT"].rearrange("(k p) m -> p k m", p=128)
        wvT = B.d["wvT"].rearrange("(k p) m -> p k m", p=128)
        wqT = B.d["wqT"].rearrange("(k p) m -> p k m", p=128)
        woT = B.d["woT"].rearrange("(k p) m -> p k m", p=128)

        def proj_tile(wslot, mc, tsl, slot):
            for k in range(KC):
                _mm(p, ps_r[:, slot, :], wa[:, wslot, k, mc * 128:(mc + 1) * 128],
                    xn[:, k, tsl], k == 0, k == KC - 1,
                    inc="mm" if k == KC - 1 else None, skip=True)

        first_kv = [True]
        n_kv = 2 * 8 * 4 * 8
        kv_done = [0]
        with _sbuf(nc, "kvf", [128, 512], F32) as kvf, \
                _sbuf(nc, "kvz", [128, 4, 8, 64], F32) as kvz:
            for mh in range(2):
                csl = slice(mh * 512, (mh + 1) * 512)
                p.wait("gpsimd", "mm", B.wfree.get("attn_w", 0))
                _g_dma(p, wa[:, 0, :, :], wkT[:, :, csl], inc="wat")
                gv = _g_dma(p, wa[:, 1, :, :], wvT[:, :, csl], inc="wat")
                p.wait("tensor", "wat", gv)
                for t in range(8):
                    tsl = slice(t * 512, (t + 1) * 512)
                    for mc in range(4):
                        m_g = mh * 4 + mc
                        p.wait("tensor", "dv", p.cnt["dv"])
                        proj_tile(0, mc, tsl, 0)
                        p.wait("scalar", "mm", p.cnt["mm"])
                        _elu1_evac(p, B, ps_r[:, 0, :], B.negbk[:, m_g:m_g + 1],
                                   B.bkc[:, m_g:m_g + 1], k_sb[:, mc, :], 512)
                        p.wait("tensor", "dv", p.cnt["dv"])
                        proj_tile(1, mc, tsl, 1)
                        p.wait("vector", "mm", p.cnt["mm"])
                        _ts(p, v_sb[:, mc, :], ps_r[:, 1, :], B.bvc[:, m_g:m_g + 1],
                            None, ALU.add, ALU.bypass, inc="dv")
                    B.wfree["attn_w"] = p.cnt["mm"]
                    p.wait("tensor", "dv", p.cnt["dv"])
                    for mc in range(4):
                        for tc in range(4):
                            _tr(p, psT[:, 0, :], k_sb[:, mc, tc * 128:(tc + 1) * 128],
                                B.ident[:, :])
                            _tr(p, psT[:, 1, :], v_sb[:, mc, tc * 128:(tc + 1) * 128],
                                B.ident[:, :], inc="mm")
                            p.wait("vector", "mm", p.cnt["mm"])
                            _cp(p, ktok[:, tc, mc, :], psT[:, 0, :])
                            dvv = _cp(p, vtok[:, tc, mc, :], psT[:, 1, :], inc="dv")
                            p.wait("tensor", "dv", dvv)
                    for tc in range(4):
                        for j in range(8):
                            h_g = mh * 8 + j
                            mc, hh = j // 2, j % 2
                            kv_done[0] += 1
                            _mm(p, ps_kv[hh * 64:(hh + 1) * 64,
                                         (h_g // 2) * 64:(h_g // 2) * 64 + 64],
                                ktok[:, tc, mc, hh * 64:(hh + 1) * 64],
                                vtok[:, tc, mc, hh * 64:(hh + 1) * 64],
                                first_kv[0], kv_done[0] == n_kv,
                                inc="mm" if kv_done[0] == n_kv else None, skip=True)
                            first_kv[0] = False
            p.wait("vector", "mm", p.cnt["mm"])
            _cp(p, kvf[:, :], ps_kv[:, :], inc="dv"); p.vs()
            kvf_v = kvf.rearrange("p (j e) -> p j e", j=8)
            for b in range(4):
                _ts(p, kvz[:, b, :, :], kvf_v[:, :, :], B.bsel_bc[:, b:b + 1], None,
                    ALU.mult, ALU.bypass, inc="dv" if b == 3 else None)
            p.wait("gpsimd", "dv", p.cnt["dv"])
            kvin = B.kv_in_d.rearrange("p (b j e) -> p b j e", b=4, j=8)
            kvout = B.kv_out_d.rearrange("p (b j e) -> p b j e", b=4, j=8)
            kq = _g_dma(p, kvin[:, :, :, :], kvz[:, :, :, :], inc="kvq")
            p.wait("gpsimd", "kvq", kq)
            p.plan("gpsimd", lambda E: E.collective_compute(
                "AllReduce", ALU.add, RG, [B.kv_in_d[:, :].opt()],
                [B.kv_out_d[:, :].opt()]), inc="cc")
            p.wait("gpsimd", "cc", p.cnt["cc"])
            p.wait("gpsimd", "mm", B.wfree.get("attn_w", 0))
            for mh in range(2):
                wav = _g_dma(p, wa[:, mh, :, :], wqT[:, :, mh * 512:(mh + 1) * 512],
                             inc="wat")
            gv = _g_dma(p, kvz[:, :, :, :], kvout[:, :, :, :], inc="kvq")
            p.wait("vector", "kvq", gv)
            _ts(p, kvf_v[:, :, :], kvz[:, 0, :, :], B.bsel_bc[:, 0:1], None,
                ALU.mult, ALU.bypass, inc="dv"); p.vs()
            for b in range(1, 4):
                _stt(p, kvf_v[:, :, :], kvz[:, b, :, :], B.bsel_bc[:, b:b + 1],
                     kvf_v[:, :, :], ALU.mult, ALU.add, inc="dv"); p.vs()
            _cp(p, kv16[:, :, :], kvf_v[:, :, :], inc="dv")
            p.wait("tensor", "wat", wav)
            if KDBG:
                p.wait("sync", "dv", p.cnt["dv"])
                _s_dma(p, B.d["dbg_kv"][:, :], kvf[:, :], inc="dbq")
                _s_dma(p, B.d["dbg_kv16"][:, :, :], kv16[:, :, :], inc="dbq")
                p.wait("sync", "dbq", p.cnt["dbq"])
                p.bar()
        p.bar()

        with _sbuf(nc, "wo_b", [128, 2, KC, 512], F16) as wo_b:
            for mh in range(2):
                wov = _g_dma(p, wo_b[:, mh, :, :], woT[:, :, mh * 512:(mh + 1) * 512],
                             inc="wob")
            p.wait("tensor", "wob", wov)
            vtok_flat = vtok.rearrange("p a b c -> p (a b c)")
            ktok_flat = ktok.rearrange("p a b c -> p (a b c)")

            def attn_sl(m, rows=slice(0, 128)):
                if m < 4:
                    return v_sb[rows, m, :]
                return vtok_flat[rows, (m - 4) * 512:(m - 3) * 512]
            for t in range(8):
                tsl = slice(t * 512, (t + 1) * 512)
                for m_g in range(8):
                    mh, mc = m_g // 4, m_g % 4
                    p.wait("tensor", "dv", p.cnt["dv"])
                    proj_tile(mh, mc, tsl, m_g % 2)
                    p.wait("scalar", "mm", p.cnt["mm"])
                    if mh == 0:
                        out16 = k_sb[:, mc, :]
                    else:
                        out16 = ktok_flat[:, mc * 512:(mc + 1) * 512]
                    _elu1_evac(p, B, ps_r[:, m_g % 2, :], B.negbq[:, m_g:m_g + 1],
                               B.bqc[:, m_g:m_g + 1], out16, 512)
                p.wait("tensor", "dv", p.cnt["dv"])
                if KDBG and t == 0:
                    for m_g in range(8):
                        src = (k_sb[:, m_g, :] if m_g < 4 else
                               ktok_flat[:, (m_g - 4) * 512:(m_g - 3) * 512])
                        _s_dma(p, B.d["dbg_q"][:, m_g, :], src, inc="dbq")
                    p.wait("sync", "dbq", p.cnt["dbq"])
                    p.bar()
                for mpair in range(8):
                    p.wait("tensor", "dv", p.cnt["dv"])
                    for hh in range(2):
                        h_g = mpair * 2 + hh
                        m_g = h_g // 2
                        if m_g < 4:
                            q_sl = k_sb[hh * 64:(hh + 1) * 64, m_g, :]
                        else:
                            q_sl = ktok_flat[hh * 64:(hh + 1) * 64,
                                             (m_g - 4) * 512:(m_g - 3) * 512]
                        _mm(p, ps_r[hh * 64:(hh + 1) * 64, 2 + hh, :],
                            kv16[hh * 64:(hh + 1) * 64, h_g // 2, :], q_sl,
                            True, True, inc="mm" if hh == 1 else None, skip=True)
                    p.wait("vector", "mm", p.cnt["mm"])
                    _ts(p, attn_sl(mpair, slice(0, 64)), ps_r[0:64, 2, :], IRS, None,
                        ALU.mult, ALU.bypass)
                    _ts(p, attn_sl(mpair, slice(64, 128)), ps_r[64:128, 3, :], IRS,
                        None, ALU.mult, ALU.bypass, inc="dv")
                p.wait("tensor", "dv", p.cnt["dv"])
                if KDBG and t == 0:
                    for m_g in range(8):
                        _s_dma(p, B.d["dbg_attn"][:, m_g, :], attn_sl(m_g), inc="dbq")
                    p.wait("sync", "dbq", p.cnt["dbq"])
                    p.bar()
                for m2 in range(8):
                    slot = m2 % 2
                    p.wait("tensor", "dv", p.cnt["dv"])
                    for k in range(KC):
                        _mm(p, ps_r[:, slot, :],
                            wo_b[:, m2 // 4, k, (m2 % 4) * 128:(m2 % 4 + 1) * 128],
                            attn_sl(k), k == 0, k == KC - 1,
                            inc="mm" if k == KC - 1 else None, skip=True)
                    p.wait("vector", "mm", p.cnt["mm"])
                    _stt(p, B.xres[:, m2, tsl], ps_r[:, slot, :], B.boS[:, m2:m2 + 1],
                         B.xres[:, m2, tsl], ALU.add, ALU.add, inc="dv")
            B.wfree["attn_w"] = p.cnt["mm"]
            p.wait("tensor", "dv", p.cnt["dv"])
    p.bar()


def _wload(p, B, key, buf, dst, s32, s16, q, par):
    if q == 0:
        p.wait("gpsimd", "mm", B.wfree.get((key, buf), 0))
        wbv = B.wbfree.get((key, buf))
        if wbv:
            p.wait("gpsimd", "wb", wbv)
        v = p.plan("gpsimd", lambda E, o=dst, i=s32: E.dma_start(out=o, in_=i),
                   inc="mwg", d=16)
        p.wait("sync", "mwg", v)
        wv = p.plan("sync", lambda E, o=s16, i=dst: E.dma_start(out=o, in_=i),
                    inc="wb", d=16)
        B.wbfree[(key, buf)] = wv
        return ("mwg", v)
    eng, semn = ("gpsimd", "mwg") if par == 0 else ("sync", "mws")
    p.wait(eng, "mm", B.wfree.get((key, buf), 0))
    v = p.plan(eng, lambda E, o=dst, i=s16: E.dma_start(out=o, in_=i),
               inc=semn, d=16)
    return (semn, v)


def emit_moe(p, B, which):
    nc = p.nc
    geglu = which == "l1"
    with contextlib.ExitStack() as es_w:
        w1b = es_w.enter_context(_sbuf(nc, "w1b", [128, 2, KC, 256], F16))
        if geglu:
            w2b = es_w.enter_context(_sbuf(nc, "w2b", [128, 2, KC, 256], F16))
        w3b = es_w.enter_context(_sbuf(nc, "w3b", [128, 4, 2, 512], F16))
        for q in range(4):
            q0 = q * 1024
            if q == 1:
                p.wait("gpsimd", "wb", p.cnt["wb"])
                p.wait("sync", "wb", p.cnt["wb"])
            with contextlib.ExitStack() as es:
                xn = es.enter_context(_sbuf(nc, "xn_q", [128, KC, 1024], F16))
                emit_ln(p, B, B.xres, q0, 1024, xn, 0, cb=True)
                cfm = es.enter_context(_sbuf(nc, "cfm", [8, 1024], F16))
                mask = es.enter_context(_sbuf(nc, "mask", [128, 2, 512], F32))
                h_sb = es.enter_context(_sbuf(nc, "h_sb", [128, 32, 1024], F16))
                if geglu:
                    tg = es.enter_context(_sbuf(nc, "tg", [128, 2, 512], F16))

                gw_l = B.gwT if which == "l0" else B.wgT
                with _psum(nc, "ps_g", [128, 2, 512], F32) as ps, \
                        _psum(nc, "psTg", [128, 4, 128], F16) as psT, \
                        _sbuf(nc, "lg16", [8, 1024], F16) as lg16, \
                        _sbuf(nc, "ltok", [128, 8, 8], F32) as ltok, \
                        _sbuf(nc, "cmb", [128, 8, 8], F32) as cmb, \
                        _sbuf(nc, "cmb16", [128, 8, 8], F16) as cmb16, \
                        _sbuf(nc, "colA", [128, 4], F32) as colA:
                    for t2 in range(2):
                        tsl = slice(t2 * 512, (t2 + 1) * 512)
                        for k in range(KC):
                            _mm(p, ps[0:8, t2, :], gw_l[:, k, :], xn[:, k, tsl],
                                k == 0, k == KC - 1, inc="mm" if k == KC - 1 else None)
                    p.wait("vector", "mm", p.cnt["mm"])
                    _cp(p, lg16[:, 0:512], ps[0:8, 0, :])
                    _cp(p, lg16[:, 512:1024], ps[0:8, 1, :], inc="dv")
                    p.wait("tensor", "dv", p.cnt["dv"])
                    for tc in range(8):
                        slot = tc % 4
                        p.wait("tensor", "dv", p.cnt["dv"])
                        _tr(p, psT[:, slot, 0:8], lg16[0:8, tc * 128:(tc + 1) * 128],
                            B.ident[0:8, 0:8], inc="mm")
                        p.wait("vector", "mm", p.cnt["mm"])
                        _cp(p, ltok[:, tc, :], psT[:, slot, 0:8], inc="dv")
                    for tc in range(8):
                        lt = ltok[:, tc, :]
                        _red(p, colA[:, 0:1], lt, ALU.max, inc="dv"); p.vs()
                        if which == "l0":
                            _ts(p, cmb[:, tc, :], lt, colA[:, 0:1], 0.0, ALU.subtract,
                                ALU.is_equal, inc="dv"); p.vs()
                            _stt(p, B.tmpf[:, 0:8], cmb[:, tc, :], -1e30, lt,
                                 ALU.mult, ALU.add, inc="dv"); p.vs()
                            _red(p, colA[:, 1:2], B.tmpf[:, 0:8], ALU.max, inc="dv"); p.vs()
                            _tt(p, colA[:, 2:3], colA[:, 0:1], colA[:, 1:2], ALU.subtract,
                                inc="dv")
                            p.wait("scalar", "dv", p.cnt["dv"])
                            _act(p, colA[:, 3:4], colA[:, 2:3], AF.Sigmoid, inc="ac")
                            p.wait("vector", "ac", p.cnt["ac"])
                            _ts(p, B.tmpf[:, 8:16], lt, colA[:, 1:2], 0.0, ALU.subtract,
                                ALU.is_equal, inc="dv"); p.vs()
                            _ts(p, colA[:, 2:3], colA[:, 3:4], -1.0, 1.0, ALU.mult,
                                ALU.add, inc="dv"); p.vs()
                            _ts(p, cmb[:, tc, :], cmb[:, tc, :], colA[:, 3:4], None,
                                ALU.mult, ALU.bypass, inc="dv"); p.vs()
                            _stt(p, cmb[:, tc, :], B.tmpf[:, 8:16], colA[:, 2:3],
                                 cmb[:, tc, :], ALU.mult, ALU.add, inc="dv"); p.vs()
                        else:
                            _ts(p, cmb[:, tc, :], lt, colA[:, 0:1], None, ALU.is_ge,
                                ALU.bypass, inc="dv"); p.vs()
                    _ts(p, cmb[:, :, :], cmb[:, :, :], IRS, None, ALU.mult, ALU.bypass,
                        inc="dv"); p.vs()
                    _cp(p, cmb16[:, :, :], cmb[:, :, :], inc="dv")
                    p.wait("tensor", "dv", p.cnt["dv"])
                    for tc in range(8):
                        slot = tc % 4
                        p.wait("tensor", "dv", p.cnt["dv"])
                        _tr(p, psT[0:8, slot, :], cmb16[:, tc, :], B.ident[:, :], inc="mm")
                        p.wait("vector", "mm", p.cnt["mm"])
                        _cp(p, cfm[:, tc * 128:(tc + 1) * 128], psT[0:8, slot, :],
                            inc="dv")
                    p.wait("tensor", "dv", p.cnt["dv"])

                with _psum(nc, "ps8", [128, 8, 512], F32) as ps8:
                    for e in range(9):
                        shared = e == 8
                        if which == "l0":
                            w1_src = B.d["l0sw1"] if shared else B.d["l0w1"][e]
                            w1_f = B.d16["l0sw1"] if shared else B.d16["l0w1"][e]
                            b1c = B.l0sb1c if shared else B.l0b1c
                            w3_src = B.d["l0sw2"] if shared else B.d["l0w2"][e]
                            w3_f = B.d16["l0sw2"] if shared else B.d16["l0w2"][e]
                            b2c = B.l0sb2c if shared else B.l0b2c
                        else:
                            w1_src = B.d["w1s"] if shared else B.d["l1w1"][e]
                            w1_f = B.d16["w1s"] if shared else B.d16["l1w1"][e]
                            w2_src = B.d["w2s"] if shared else B.d["l1w2"][e]
                            w2_f = B.d16["w2s"] if shared else B.d16["l1w2"][e]
                            w3_src = B.d["w3s"] if shared else B.d["l1w3"][e]
                            w3_f = B.d16["w3s"] if shared else B.d16["l1w3"][e]
                        w1r = w1_src.rearrange("(k p) m -> p k m", p=128)
                        w1fr = w1_f.rearrange("(k p) m -> p k m", p=128)
                        if geglu:
                            w2r = w2_src.rearrange("(k p) m -> p k m", p=128)
                            w2fr = w2_f.rearrange("(k p) m -> p k m", p=128)
                        w3r = w3_src.rearrange("(k p) m -> p k m", p=128)
                        w3fr = w3_f.rearrange("(k p) m -> p k m", p=128)

                        p.wait("tensor", "dv", p.cnt["dv"])
                        if not shared:
                            for t2 in range(2):
                                _mm(p, ps8[:, 4 + t2, :], B.sel8[:, e * 128:(e + 1) * 128],
                                    cfm[0:8, t2 * 512:(t2 + 1) * 512], True, True,
                                    inc="mm" if t2 == 1 else None)
                            p.wait("vector", "mm", p.cnt["mm"])
                            _cp(p, mask[:, 0, :], ps8[:, 4, :])
                            mdv = _cp(p, mask[:, 1, :], ps8[:, 5, :], inc="dv")
                            slotgate = {4: ("dv", mdv), 5: ("dv", mdv)}
                        else:
                            slotgate = {}
                        tggate = {}

                        for c in range(16):
                            buf = c % 2
                            csl = slice(c * 256, (c + 1) * 256)
                            if geglu:
                                s1, v1 = _wload(p, B, "w1", buf, w1b[:, buf, :, :],
                                                w1r[:, :, csl], w1fr[:, :, csl], q, 0)
                                s2, v2 = _wload(p, B, "w2", buf, w2b[:, buf, :, :],
                                                w2r[:, :, csl], w2fr[:, :, csl], q, 1)
                                p.wait("tensor", s1, v1)
                                p.wait("tensor", s2, v2)
                            else:
                                s1, v1 = _wload(p, B, "w1", buf, w1b[:, buf, :, :],
                                                w1r[:, :, csl], w1fr[:, :, csl], q,
                                                c % 2)
                                p.wait("tensor", s1, v1)
                            for mc in range(2):
                                ch = c * 2 + mc
                                for t2 in range(2):
                                    tsl = slice(t2 * 512, (t2 + 1) * 512)
                                    b1 = mc * 2 + t2
                                    if b1 in slotgate:
                                        p.wait("tensor", *slotgate[b1])
                                    if geglu and (4 + b1) in slotgate:
                                        p.wait("tensor", *slotgate[4 + b1])
                                    for k in range(KC):
                                        _mm(p, ps8[:, b1, :],
                                            w1b[:, buf, k, mc * 128:(mc + 1) * 128],
                                            xn[:, k, tsl], k == 0, k == KC - 1,
                                            inc="mm" if (k == KC - 1 and not geglu)
                                            else None, skip=True)
                                    if geglu:
                                        for k in range(KC):
                                            _mm(p, ps8[:, 4 + b1, :],
                                                w2b[:, buf, k, mc * 128:(mc + 1) * 128],
                                                xn[:, k, tsl], k == 0, k == KC - 1,
                                                inc="mm" if k == KC - 1 else None,
                                                skip=True)
                                    mmv = p.cnt["mm"]
                                    p.wait("scalar", "mm", mmv)
                                    if which == "l0":
                                        bias = (b1c[:, ch:ch + 1] if shared
                                                else b1c[:, ch, e:e + 1])
                                        av = _act(p, h_sb[:, ch, tsl], ps8[:, b1, :],
                                                  AF.Gelu, bias=bias, inc="ac")
                                        slotgate[b1] = ("ac", av)
                                    else:
                                        if t2 in tggate:
                                            p.wait("scalar", "dv", tggate[t2])
                                        av = _act(p, tg[:, t2, :], ps8[:, b1, :],
                                                  AF.Gelu, inc="ac")
                                        slotgate[b1] = ("ac", av)
                                        p.wait("vector", "ac", av)
                                        dvv = _tt(p, h_sb[:, ch, tsl], tg[:, t2, :],
                                                  ps8[:, 4 + b1, :], ALU.mult, inc="dv")
                                        slotgate[4 + b1] = ("dv", dvv)
                                        tggate[t2] = dvv
                            B.wfree[("w1", buf)] = p.cnt["mm"]
                            if geglu:
                                B.wfree[("w2", buf)] = p.cnt["mm"]

                        p.wait("tensor", "ac", p.cnt["ac"])
                        p.wait("tensor", "dv", p.cnt["dv"])
                        for mh in range(2):
                            if mh == 1:
                                p.wait("tensor", "dv", p.cnt["dv"])
                            for kg in range(16):
                                buf = kg % 4
                                ssl = (slice(kg * 2, (kg + 1) * 2),
                                       slice(mh * 512, (mh + 1) * 512))
                                s3, v3 = _wload(p, B, "w3", buf, w3b[:, buf, :, :],
                                                w3r[:, ssl[0], ssl[1]],
                                                w3fr[:, ssl[0], ssl[1]], q, kg % 2)
                                p.wait("tensor", s3, v3)
                                for m4 in range(4):
                                    for k in range(2):
                                        for t2 in range(2):
                                            last = (m4 == 3 and k == 1 and t2 == 1)
                                            _mm(p, ps8[:, m4 * 2 + t2, :],
                                                w3b[:, buf, k, m4 * 128:(m4 + 1) * 128],
                                                h_sb[:, kg * 2 + k,
                                                     t2 * 512:(t2 + 1) * 512],
                                                kg == 0 and k == 0, kg == 15 and k == 1,
                                                inc="mm" if last else None, skip=True)
                                B.wfree[("w3", buf)] = p.cnt["mm"]
                            p.wait("vector", "mm", p.cnt["mm"])
                            for m4 in range(4):
                                for t2 in range(2):
                                    m = mh * 4 + m4
                                    psym = ps8[:, m4 * 2 + t2, :]
                                    xsl = B.xres[:, m, q0 + t2 * 512:q0 + (t2 + 1) * 512]
                                    if shared:
                                        if which == "l0":
                                            _ts(p, B.tmpf[:, 0:512], psym, IRS,
                                                B.l0sb2S[:, m:m + 1], ALU.mult, ALU.add)
                                            _tt(p, xsl, xsl, B.tmpf[:, 0:512], ALU.add,
                                                inc="dv")
                                        else:
                                            _stt(p, xsl, psym, IRS, xsl,
                                                 ALU.mult, ALU.add, inc="dv")
                                    else:
                                        if which == "l0":
                                            _stt(p, B.tmpf[:, 0:512], psym,
                                                 b2c[:, m, e:e + 1], mask[:, t2, :],
                                                 ALU.add, ALU.mult)
                                        else:
                                            _stt(p, B.tmpf[:, 0:512], psym, 1.0,
                                                 mask[:, t2, :], ALU.mult, ALU.mult)
                                        _tt(p, xsl, xsl, B.tmpf[:, 0:512], ALU.add,
                                            inc="dv")
    p.bar()



def emit_halo_exchange(p, B):
    nc = p.nc
    with _sbuf(nc, "h32", [128, KC, 3], F32) as h32, \
            _sbuf(nc, "hg", [128, 8, KC, 3], F32) as hg:
        for k in range(KC):
            _cp(p, h32[:, k, :], B.xres[:, k, TOK - 3:TOK],
                inc="dv" if k == KC - 1 else None)
        p.wait("gpsimd", "dv", p.cnt["dv"])
        hin = B.halo_in_d.rearrange("(k p) c -> p k c", p=128)
        hq = _g_dma(p, hin[:, :, :], h32[:, :, :], inc="hlq")
        p.wait("gpsimd", "hlq", hq)
        p.plan("gpsimd", lambda E: E.collective_compute(
            "AllGather", ALU.bypass, RG, [B.halo_in_d[:, :].opt()],
            [B.halo_out_d[:, :].opt()]), inc="cc")
        p.wait("gpsimd", "cc", p.cnt["cc"])
        hout = B.halo_out_d.rearrange("(r k p) c -> p r k c", p=128, k=KC)
        gv = _g_dma(p, hg[:, :, :, :], hout[:, :, :, :], inc="hlq")
        p.wait("vector", "hlq", gv)
        _ts(p, h32[:, :, :], hg[:, 0, :, :], B.hsel_bc[:, 0:1], None,
            ALU.mult, ALU.bypass, inc="dv"); p.vs()
        for r in range(1, 8):
            _stt(p, h32[:, :, :], hg[:, r, :, :], B.hsel_bc[:, r:r + 1],
                 h32[:, :, :], ALU.mult, ALU.add, inc="dv"); p.vs()
        p.plan("vector", lambda E, t=B.halo16: E.memset(t[:, :, :], 0.0), inc="dv")
        p.vs()
        _cp(p, B.halo16[:, :, 0:3], h32[:, :, :], inc="dv")
        if KDBG:
            p.wait("sync", "dv", p.cnt["dv"])
            _s_dma(p, B.d["dbg_h16"][:, :, :], B.halo16[:, :, :], inc="dbq")
            _s_dma(p, B.d["dbg_hsel"][:, :], B.hsel_bc[:, :], inc="dbq")
            p.wait("sync", "dbq", p.cnt["dbq"])
        p.wait("tensor", "dv", p.cnt["dv"])
    p.bar()


def emit_l1_attn(p, B):
    nc = p.nc
    CH = 16
    emit_halo_exchange(p, B)
    winT = B.d["winT"].rearrange("(k p) m -> p k m", p=128)
    woutT = B.d["woutT"].rearrange("(k p) m -> p k m", p=128)
    for hf in range(2):
        h0 = hf * 2048
        hoff = 3 if hf == 0 else 0
        with contextlib.ExitStack() as es:
            xn = es.enter_context(_sbuf(nc, "xn_h", [128, KC, 2051], F16))
            if hf == 0:
                emit_ln(p, B, B.halo16, 0, 3, xn, 0, pad=128)
                if KDBG:
                    _s_dma(p, B.d["dbg_srow"][:, :, :], B.srow[:, :, :], inc="dbq")
                    for k in range(KC):
                        _s_dma(p, B.d["dbg_xnh"][:, k, 0:3], xn[:, k, 0:3], inc="dbq")
                    p.wait("sync", "dbq", p.cnt["dbq"])
                    p.bar()
            emit_ln(p, B, B.xres, h0, 2048, xn, hoff)

            xc = es.enter_context(_sbuf(nc, "xc", [128, CH, 2048], F16))
            es_in = es.enter_context(contextlib.ExitStack())
            xs = es_in.enter_context(_sbuf(nc, "xs", [128, 2051], F16))
            wib = es_in.enter_context(_sbuf(nc, "wib", [128, 2, KC, 128], F16))
            dt16 = es_in.enter_context(_sbuf(nc, "dt16", [32, 2048], F16))
            abuf = es_in.enter_context(_sbuf(nc, "abuf", [128, 2048], F16))
            sz16 = es_in.enter_context(_sbuf(nc, "sz16", [128, 512], F16))
            with _psum(nc, "ps_m", [128, 4, 512], F32) as ps:
                ntok_xs = 2048 + hoff
                for ch in range(CH):
                    buf = ch % 2
                    p.wait("gpsimd", "dv", B.wfree.get(("wi", buf), 0))
                    gwv = _g_dma(p, wib[:, buf, :, :],
                                 winT[:, :, ch * 128:(ch + 1) * 128],
                                 inc="wi%d" % buf)
                    p.wait("tensor", "wi%d" % buf, gwv)
                    if hf == 1:
                        p.wait("vector", "ac", p.cnt["ac"])
                        _cp(p, xs[:, 0:3], B.xs_keep[:, ch, :], inc="dv")
                    nt = 0
                    while nt < ntok_xs:
                        w = min(512, ntok_xs - nt)
                        slot = (nt // 512) % 4
                        if ("xp", slot) in B.pgate:
                            p.wait("tensor", *B.pgate[("xp", slot)])
                        for k in range(KC):
                            _mm(p, ps[:, slot, 0:w], wib[:, buf, k, :],
                                xn[:, k, nt:nt + w], k == 0, k == KC - 1,
                                inc="mm" if k == KC - 1 else None)
                        p.wait("vector", "mm", p.cnt["mm"])
                        dvv = _cp(p, xs[:, (3 - hoff) + nt:(3 - hoff) + nt + w],
                                  ps[:, slot, 0:w], inc="dv")
                        B.pgate[("xp", slot)] = ("dv", dvv)
                        B.wfree[("wi", buf)] = dvv
                        nt += w
                    if hf == 0:
                        _cp(p, B.xs_keep[:, ch, :], xs[:, 2048:2051], inc="dv")
                    if KDBG and hf == 0 and ch == 0:
                        p.wait("sync", "dv", p.cnt["dv"])
                        _s_dma(p, B.d["dbg_xs"][:, :], xs[:, :], inc="dbq")
                        p.wait("sync", "dbq", p.cnt["dbq"])
                        p.bar()
                    for t in range(4):
                        tsl = slice(t * 512, (t + 1) * 512)
                        p.wait("vector", "dv", p.cnt["dv"])
                        p.wait("vector", "ac", p.cnt["ac"])
                        _ts(p, B.tmpf[:, 0:512], xs[:, t * 512:t * 512 + 512],
                            B.convwc[:, ch, 0:1], None, ALU.mult, ALU.bypass,
                            inc="dv"); p.vs()
                        for j in range(1, 4):
                            _stt(p, B.tmpf[:, 0:512],
                                 xs[:, t * 512 + j:t * 512 + j + 512],
                                 B.convwc[:, ch, j:j + 1], B.tmpf[:, 0:512],
                                 ALU.mult, ALU.add, inc="dv"); p.vs()
                        p.wait("scalar", "dv", p.cnt["dv"])
                        _act(p, xc[:, ch, tsl], B.tmpf[:, 0:512], AF.Silu,
                             bias=B.convbc[:, ch:ch + 1], inc="ac")
                p.wait("tensor", "ac", p.cnt["ac"])
                if KDBG and hf == 0:
                    p.wait("sync", "ac", p.cnt["ac"])
                    for ch in range(CH):
                        _s_dma(p, B.d["dbg_xc"][:, ch, :], xc[:, ch, :], inc="dbq")
                    p.wait("sync", "dbq", p.cnt["dbq"])
                    p.bar()
                for t in range(4):
                    tsl = slice(t * 512, (t + 1) * 512)
                    slot = t % 2
                    p.wait("tensor", "dv", p.cnt["dv"])
                    for k in range(CH):
                        _mm(p, ps[0:32, slot, :], B.wxT[:, k, :], xc[:, k, tsl],
                            k == 0, k == CH - 1, inc="mm" if k == CH - 1 else None)
                    p.wait("vector", "mm", p.cnt["mm"])
                    _cp(p, dt16[:, tsl], ps[0:32, slot, :], inc="dv")
                if KDBG and hf == 0:
                    p.wait("sync", "dv", p.cnt["dv"])
                    _s_dma(p, B.d["dbg_dt"][:, :], dt16[:, :], inc="dbq")
                    p.wait("sync", "dbq", p.cnt["dbq"])
                p.bar()
                for ch in range(CH):
                    buf = ch % 2
                    for t in range(4):
                        tsl = slice(t * 512, (t + 1) * 512)
                        slot = t % 2
                        p.wait("tensor", "ac", p.cnt["ac"])
                        _mm(p, ps[:, slot, :], B.wdtT[:, ch * 128:(ch + 1) * 128],
                            dt16[:, tsl], True, True, inc="mm")
                        p.wait("scalar", "mm", p.cnt["mm"])
                        p.wait("scalar", "dv", p.cnt["dv"])
                        _act(p, B.e1[:, 0, :], ps[:, slot, :], AF.Abs,
                             bias=B.bdtc[:, ch:ch + 1])
                        _act(p, B.e1[:, 0, :], B.e1[:, 0, :], AF.Exp, scale=-1.0)
                        _act(p, B.e1[:, 0, :], B.e1[:, 0, :], AF.Ln, bias=B.c_one[:, 0:1])
                        _act(p, B.e2[:, 0, :], ps[:, slot, :], AF.Relu,
                             bias=B.bdtc[:, ch:ch + 1], inc="ac")
                        p.wait("vector", "ac", p.cnt["ac"])
                        _tt(p, abuf[:, tsl], B.e1[:, 0, :], B.e2[:, 0, :], ALU.add,
                            inc="dv")
                    if KDBG and hf == 0 and ch == 0:
                        p.wait("sync", "dv", p.cnt["dv"])
                        _s_dma(p, B.d["dbg_ab"][:, :], abuf[:, :], inc="dbq")
                        p.wait("sync", "dbq", p.cnt["dbq"])
                        p.bar()
                    p.wait("gpsimd", "dv", B.wfree.get(("wi", buf), 0))
                    gwv = _g_dma(p, wib[:, buf, :, :],
                                 winT[:, :, 2048 + ch * 128:2048 + (ch + 1) * 128],
                                 inc="wi%d" % buf)
                    p.wait("tensor", "wi%d" % buf, gwv)
                    for t in range(4):
                        tsl = slice(t * 512, (t + 1) * 512)
                        slot = 2 + t % 2
                        p.wait("tensor", "ac", p.cnt["ac"])
                        p.wait("tensor", "dv", p.cnt["dv"])
                        for k in range(KC):
                            _mm(p, ps[:, slot, :], wib[:, buf, k, :],
                                xn[:, k, hoff + t * 512:hoff + t * 512 + 512],
                                k == 0, k == KC - 1, inc="mm" if k == KC - 1 else None)
                        p.wait("scalar", "mm", p.cnt["mm"])
                        _act(p, sz16[:, :], ps[:, slot, :], AF.Silu, inc="ac")
                        p.wait("vector", "ac", p.cnt["ac"])
                        _tt(p, B.tmpf[:, 0:512], abuf[:, tsl], xc[:, ch, tsl], ALU.mult)
                        dvv = _tt(p, xc[:, ch, tsl], B.tmpf[:, 0:512], sz16[:, :],
                                  ALU.mult, inc="dv")
                        B.wfree[("wi", buf)] = dvv
                p.bar()
                es_in.close()
                with _sbuf(nc, "wob", [128, 2, CH, 128], F16) as wob:
                    for m in range(8):
                        buf = m % 2
                        p.wait("gpsimd", "dv", B.wfree.get(("wo2", buf), 0))
                        gwv = _g_dma(p, wob[:, buf, :, :],
                                     woutT[:, :, m * 128:(m + 1) * 128],
                                     inc="wo2%d" % buf)
                        p.wait("tensor", "wo2%d" % buf, gwv)
                        for t in range(4):
                            tsl = slice(t * 512, (t + 1) * 512)
                            slot = t % 4
                            p.wait("tensor", "dv", p.cnt["dv"])
                            for k in range(CH):
                                _mm(p, ps[:, slot, :], wob[:, buf, k, :], xc[:, k, tsl],
                                    k == 0, k == CH - 1,
                                    inc="mm" if k == CH - 1 else None)
                            p.wait("vector", "mm", p.cnt["mm"])
                            xsl = B.xres[:, m, h0 + t * 512:h0 + (t + 1) * 512]
                            _stt(p, B.tmpf[:, 0:512], ps[:, slot, :], 1.0,
                                 xn[:, m, hoff + t * 512:hoff + t * 512 + 512],
                                 ALU.mult, ALU.add)
                            dvv = _stt(p, xsl, B.tmpf[:, 0:512], IRS, xsl,
                                       ALU.mult, ALU.add, inc="dv")
                            B.wfree[("wo2", buf)] = dvv
                    p.wait("tensor", "dv", p.cnt["dv"])
        p.bar()


def emit_store(p, B):
    nc = p.nc
    with _sbuf(nc, "stage", [128, 4, 512], F32) as stage:
        xo = B.d["xo"].rearrange("(k p) n -> p k n", p=128)
        for k in range(KC):
            for t in range(8):
                i = k * 8 + t
                slot = i % 4
                if i >= 4:
                    p.wait("vector", "st%d" % slot, p.cnt["st%d" % slot])
                _ts(p, stage[:, slot, :], B.xres[:, k, t * 512:(t + 1) * 512],
                    RS, None, ALU.mult, ALU.bypass, inc="dv")
                p.wait("sync", "dv", p.cnt["dv"])
                _s_dma(p, xo[:, k, t * 512:(t + 1) * 512], stage[:, slot, :],
                       inc="st%d" % slot)
        for sl in range(4):
            p.wait("sync", "st%d" % sl, p.cnt["st%d" % sl])
            p.wait("gpsimd", "st%d" % sl, p.cnt["st%d" % sl])




class Bag:
    pass


def build_program(blocks):
    nc = bass.Bass("TRN2", target_bir_lowering=False)
    B = Bag()
    B.wfree = {}
    B.wbfree = {}
    B.pgate = {}
    B.egate = {}
    d = {}
    d16 = {}
    need = set(blocks)

    def d16t(name, shape):
        d16[name] = nc.dram_tensor(name + "_f16", list(shape), F16,
                                   kind="Internal").ap()

    def din(name, shape):
        d[name] = nc.dram_tensor(name, list(shape), F32, kind="ExternalInput").ap()

    din("x", (DM, TOK))
    din("ident", (128, 128))
    din("sel8", (8, 1024))
    din("bsel", (1, 4))
    din("hsel", (1, 8))
    if 0 in need:
        for n in ("wqT", "wkT", "wvT", "woT"):
            din(n, (DM, DM))
        for n in ("bq", "bk", "bv", "bo"):
            din(n, (128, 8))
    if 1 in need:
        din("gwT", (128, 8, 8))
        din("l0w1", (8, DM, 4096)); din("l0b1", (128, 32, 8))
        din("l0w2", (8, 4096, DM)); din("l0b2", (128, 8, 8))
        din("l0sw1", (DM, 4096)); din("l0sb1", (128, 32))
        din("l0sw2", (4096, DM)); din("l0sb2", (128, 8))
        d16t("l0w1", (8, DM, 4096)); d16t("l0w2", (8, 4096, DM))
        d16t("l0sw1", (DM, 4096)); d16t("l0sw2", (4096, DM))
    if 2 in need:
        din("winT", (DM, 4096))
        din("convw", (128, 16, 4)); din("convb", (128, 16))
        din("wxT", (128, 16, 32)); din("wdtT", (32, 2048)); din("bdt", (128, 16))
        din("woutT", (2048, DM))
    if 3 in need:
        din("wgT", (128, 8, 8))
        din("w1s", (DM, 4096)); din("w2s", (DM, 4096)); din("w3s", (4096, DM))
        din("l1w1", (8, DM, 4096)); din("l1w2", (8, DM, 4096))
        din("l1w3", (8, 4096, DM))
        d16t("w1s", (DM, 4096)); d16t("w2s", (DM, 4096)); d16t("w3s", (4096, DM))
        d16t("l1w1", (8, DM, 4096)); d16t("l1w2", (8, DM, 4096))
        d16t("l1w3", (8, 4096, DM))
    d["xo"] = nc.dram_tensor("xo", [DM, TOK], F32, kind="ExternalOutput").ap()
    if KDBG:
        d["dbg_xn"] = nc.dram_tensor("dbg_xn", [DM, TOK], F16, kind="ExternalOutput").ap()
        d["dbg_kv"] = nc.dram_tensor("dbg_kv", [128, 512], F32, kind="ExternalOutput").ap()
        d["dbg_kv16"] = nc.dram_tensor("dbg_kv16", [128, 8, 64], F16, kind="ExternalOutput").ap()
        d["dbg_q"] = nc.dram_tensor("dbg_q", [128, 8, 512], F16, kind="ExternalOutput").ap()
        d["dbg_attn"] = nc.dram_tensor("dbg_attn", [128, 8, 512], F16, kind="ExternalOutput").ap()
        d["dbg_lg"] = nc.dram_tensor("dbg_lg", [8, 1024], F32, kind="ExternalOutput").ap()
        d["dbg_cfm"] = nc.dram_tensor("dbg_cfm", [8, 1024], F16, kind="ExternalOutput").ap()
        d["dbg_h"] = nc.dram_tensor("dbg_h", [128, 32, 1024], F16, kind="ExternalOutput").ap()
        d["dbg_ltok"] = nc.dram_tensor("dbg_ltok", [128, 8, 8], F32, kind="ExternalOutput").ap()
        d["dbg_cmb"] = nc.dram_tensor("dbg_cmb", [128, 8, 8], F32, kind="ExternalOutput").ap()
        d["dbg_xnh"] = nc.dram_tensor("dbg_xnh", [128, 8, 2051], F16, kind="ExternalOutput").ap()
        d["dbg_xs"] = nc.dram_tensor("dbg_xs", [128, 2051], F16, kind="ExternalOutput").ap()
        d["dbg_xc"] = nc.dram_tensor("dbg_xc", [128, 16, 2048], F16, kind="ExternalOutput").ap()
        d["dbg_dt"] = nc.dram_tensor("dbg_dt", [32, 2048], F16, kind="ExternalOutput").ap()
        d["dbg_ab"] = nc.dram_tensor("dbg_ab", [128, 2048], F16, kind="ExternalOutput").ap()
        d["dbg_h16"] = nc.dram_tensor("dbg_h16", [128, 8, 128], F16, kind="ExternalOutput").ap()
        d["dbg_hsel"] = nc.dram_tensor("dbg_hsel", [128, 8], F32, kind="ExternalOutput").ap()
        d["dbg_srow"] = nc.dram_tensor("dbg_srow", [1, 3, 512], F32, kind="ExternalOutput").ap()
    B.d = d
    B.d16 = d16
    if 0 in need:
        B.kv_in_d = nc.dram_tensor("kv_in_d", [128, 2048], F32, kind="Internal").ap()
        B.kv_out_d = nc.dram_tensor("kv_out_d", [128, 2048], F32, kind="Internal").ap()
    if 2 in need:
        B.halo_in_d = nc.dram_tensor("halo_in_d", [DM, 3], F32, kind="Internal").ap()
        B.halo_out_d = nc.dram_tensor("halo_out_d", [8 * DM, 3], F32,
                                      kind="Internal").ap()

    with contextlib.ExitStack() as es:
        p = Plan(nc, es)
        B.xres = es.enter_context(_sbuf(nc, "xres", [128, KC, TOK], F16))
        B.srow = es.enter_context(_sbuf(nc, "srow", [1, 3, 512], F32))
        B.row16 = es.enter_context(_sbuf(nc, "row16", [1, 2, 512], F16))
        B.tmpf = es.enter_context(_sbuf(nc, "tmpf", [128, 512], F32))
        B.e1 = es.enter_context(_sbuf(nc, "e1", [128, 2, 512], F16))
        B.e2 = es.enter_context(_sbuf(nc, "e2", [128, 2, 512], F16))
        B.ones_col = es.enter_context(_sbuf(nc, "ones_col", [128, 1], F16))
        B.c_eps = es.enter_context(_sbuf(nc, "c_eps", [1, 1], F32))
        B.c_one = es.enter_context(_sbuf(nc, "c_one", [128, 1], F32))
        B.ones_row = es.enter_context(_sbuf(nc, "ones_row", [1, 128], F16))
        B.ident = es.enter_context(_sbuf(nc, "identS", [128, 128], F16))
        B.sel8 = es.enter_context(_sbuf(nc, "sel8S", [8, 1024], F16))
        B.bselr = es.enter_context(_sbuf(nc, "bselr", [1, 4], F32))
        B.hselr = es.enter_context(_sbuf(nc, "hselr", [1, 8], F32))
        B.bsel_bc = es.enter_context(_sbuf(nc, "bsel_bc", [128, 4], F32))
        B.hsel_bc = es.enter_context(_sbuf(nc, "hsel_bc", [128, 8], F32))
        if 0 in need:
            for n in ("bqc", "bkc", "bvc", "boc", "negbq", "negbk", "boS"):
                setattr(B, n, es.enter_context(_sbuf(nc, n, [128, 8], F32)))
        if 1 in need:
            B.gwT = es.enter_context(_sbuf(nc, "gwTs", [128, 8, 8], F16))
            B.l0b1c = es.enter_context(_sbuf(nc, "l0b1c", [128, 32, 8], F32))
            B.l0b2c = es.enter_context(_sbuf(nc, "l0b2c", [128, 8, 8], F32))
            B.l0sb1c = es.enter_context(_sbuf(nc, "l0sb1c", [128, 32], F32))
            B.l0sb2c = es.enter_context(_sbuf(nc, "l0sb2c", [128, 8], F32))
            B.l0sb2S = es.enter_context(_sbuf(nc, "l0sb2S", [128, 8], F32))
        if 2 in need:
            B.convwc = es.enter_context(_sbuf(nc, "convwc", [128, 16, 4], F32))
            B.convbc = es.enter_context(_sbuf(nc, "convbc", [128, 16], F32))
            B.bdtc = es.enter_context(_sbuf(nc, "bdtc", [128, 16], F32))
            B.wxT = es.enter_context(_sbuf(nc, "wxTs", [128, 16, 32], F16))
            B.wdtT = es.enter_context(_sbuf(nc, "wdtTs", [32, 2048], F16))
            B.halo16 = es.enter_context(_sbuf(nc, "halo16", [128, KC, 128], F16))
            B.xs_keep = es.enter_context(_sbuf(nc, "xs_keep", [128, 16, 3], F16))
        if 3 in need:
            B.wgT = es.enter_context(_sbuf(nc, "wgTs", [128, 8, 8], F16))

        with nc.Block() as blk:
            emit_init(p, B, need)
            for b in blocks:
                if b == 0:
                    emit_l0_attn(p, B)
                elif b == 1:
                    emit_moe(p, B, "l0")
                elif b == 2:
                    emit_l1_attn(p, B)
                elif b == 3:
                    emit_moe(p, B, "l1")
            emit_store(p, B)

            @blk.sync
            def _(E):
                for f in p.streams["sync"]:
                    f(E)

            @blk.tensor
            def _(E):
                for f in p.streams["tensor"]:
                    f(E)

            @blk.vector
            def _(E):
                for f in p.streams["vector"]:
                    f(E)

            @blk.scalar
            def _(E):
                for f in p.streams["scalar"]:
                    f(E)

            @blk.gpsimd
            def _(E):
                for f in p.streams["gpsimd"]:
                    f(E)
    return nc


def _prep_weights(I):
    g = {}
    T = lambda a: np.ascontiguousarray(np.asarray(a).T.astype(np.float32))
    C = lambda a: np.ascontiguousarray(np.asarray(a, dtype=np.float32))
    g["wqT"] = T(I["L0_attn_Wq"]); g["wkT"] = T(I["L0_attn_Wk"])
    g["wvT"] = T(I["L0_attn_Wv"]); g["woT"] = T(I["L0_attn_Wo"])
    for n, src in (("bq", "L0_attn_bq"), ("bk", "L0_attn_bk"),
                   ("bv", "L0_attn_bv"), ("bo", "L0_attn_bo")):
        g[n] = C(np.asarray(I[src]).reshape(8, 128).T)
    g["gwT"] = C(np.asarray(I["L0_ffn_gate_w"]).T.reshape(8, 128, 8).transpose(1, 0, 2))
    g["l0w1"] = C(I["L0_ffn_W1"])
    g["l0b1"] = C(np.asarray(I["L0_ffn_b1"]).reshape(8, 32, 128).transpose(2, 1, 0))
    g["l0w2"] = C(I["L0_ffn_W2"])
    g["l0b2"] = C(np.asarray(I["L0_ffn_b2"]).reshape(8, 8, 128).transpose(2, 1, 0))
    g["l0sw1"] = C(I["L0_ffn_sW1"])
    g["l0sb1"] = C(np.asarray(I["L0_ffn_sb1"]).reshape(32, 128).T)
    g["l0sw2"] = C(I["L0_ffn_sW2"])
    g["l0sb2"] = C(np.asarray(I["L0_ffn_sb2"]).reshape(8, 128).T)
    g["winT"] = T(I["L1_attn_W_in"])
    g["convw"] = C(np.asarray(I["L1_attn_conv_w"])[:, 0, :].reshape(16, 128, 4)
                   .transpose(1, 0, 2))
    g["convb"] = C(np.asarray(I["L1_attn_conv_b"]).reshape(16, 128).T)
    g["wxT"] = C(np.asarray(I["L1_attn_W_x"])[0:32, :].T.reshape(16, 128, 32)
                 .transpose(1, 0, 2))
    g["wdtT"] = T(I["L1_attn_W_dt"])
    g["bdt"] = C(np.asarray(I["L1_attn_b_dt"]).reshape(16, 128).T)
    g["woutT"] = T(I["L1_attn_W_out"])
    g["w1s"] = C(I["L1_ffn_w1s"]); g["w2s"] = C(I["L1_ffn_w2s"])
    g["w3s"] = C(I["L1_ffn_w3s"])
    g["l1w1"] = C(I["L1_ffn_W1"]); g["l1w2"] = C(I["L1_ffn_W2"])
    g["l1w3"] = C(I["L1_ffn_W3"])
    g["wgT"] = C(np.asarray(I["L1_ffn_Wg"]).reshape(8, 128, 8).transpose(1, 0, 2))
    g["ident"] = np.eye(128, dtype=np.float32)
    sel8 = np.zeros((8, 1024), dtype=np.float32)
    for e in range(8):
        sel8[e, e * 128:(e + 1) * 128] = 1.0
    g["sel8"] = sel8
    return g


_INPUT_KEYS = {
    0: ["wqT", "wkT", "wvT", "woT", "bq", "bk", "bv", "bo"],
    1: ["gwT", "l0w1", "l0b1", "l0w2", "l0b2", "l0sw1", "l0sb1", "l0sw2", "l0sb2"],
    2: ["winT", "convw", "convb", "wxT", "wdtT", "bdt", "woutT"],
    3: ["wgT", "w1s", "w2s", "w3s", "l1w1", "l1w2", "l1w3"],
}


def _run_blocks(inputs, launch_groups):
    g = _prep_weights(inputs)
    x = np.asarray(inputs["L0_attn_x"], dtype=np.float32)
    shards = []
    for c in range(NCORES):
        b, h = c // 2, c % 2
        shards.append(np.ascontiguousarray(x[b, h * TOK:(h + 1) * TOK, :].T))
    for grp in launch_groups:
        nc = build_program(grp)
        keys = ["ident", "sel8"]
        for b in grp:
            keys += _INPUT_KEYS[b]
        in_maps = []
        for c in range(NCORES):
            bsel = np.zeros((1, 4), dtype=np.float32); bsel[0, c // 2] = 1.0
            hsel = np.zeros((1, 8), dtype=np.float32)
            if c % 2 == 1:
                hsel[0, c - 1] = 1.0
            m = {k: g[k] for k in keys}
            m["x"] = shards[c]
            m["bsel"] = bsel
            m["hsel"] = hsel
            in_maps.append(m)
        res = run_bass_kernel_spmd(nc, in_maps, core_ids=list(range(NCORES)))
        shards = [np.ascontiguousarray(r["xo"]) for r in res.results]
    out = np.empty((4, 8192, DM), dtype=np.float32)
    for c in range(NCORES):
        b, h = c // 2, c % 2
        out[b, h * TOK:(h + 1) * TOK, :] = shards[c].T
    return out


def kernel(**inputs):
    return _run_blocks(inputs, LAUNCH_GROUPS)
